# version 08 (final)
# baseline (speedup 1.0000x reference)
import numpy as np
import ml_dtypes
from contextlib import ExitStack

import concourse.bass as bass
import concourse.mybir as mybir
from concourse.bass_utils import run_bass_kernel_spmd
from concourse.expressions import RuntimeValue

F32 = mybir.dt.float32
BF16 = mybir.dt.bfloat16
I32 = mybir.dt.int32
AF = mybir.ActivationFunctionType
ALU = mybir.AluOpType
AX = mybir.AxisListType
ds = bass.ds
BF16NP = ml_dtypes.bfloat16

NCORES = 8
T = 4096
D = 2048
H = 16
HD = 128
NT = T // 128
TB = 256
NB = T // TB
NCH = D // 128
LN2 = 0.6931471805599453
TINY = 1.1754943508222875e-38


class EW:

    def __init__(self, eng):
        self.e = eng
        self._dw = []
        self._dr = []

    @staticmethod
    def _foot(ap):
        t = getattr(ap, "tensor", None)
        name = t.name if t is not None else getattr(ap, "name", repr(ap))
        try:
            off = int(ap.offset)
            apl = [(int(s), int(c)) for s, c in ap.ap]
            span = 1
            for s, c in apl[1:]:
                span += abs(s) * (c - 1)
            if len(apl) == 1:
                span = 1
            return (name, off, off + span)
        except Exception:
            return (name, 0, 1 << 60)

    @classmethod
    def _feet(cls, aps):
        return [cls._foot(ap) for ap in aps
                if ap is not None and not isinstance(ap, (int, float))]

    @staticmethod
    def _overlap(a, b):
        for (n1, l1, h1) in a:
            for (n2, l2, h2) in b:
                if n1 == n2 and l1 < h2 and l2 < h1:
                    return True
        return False

    def _emit(self, reads, writes, custom, fn):
        rf = self._feet(reads)
        wf = self._feet(writes)
        if (self._overlap(rf, self._dw) or self._overlap(wf, self._dw)
                or self._overlap(wf, self._dr)):
            self.e.drain()
            self._dw = []
            self._dr = []
        r = fn()
        self._dw += wf
        self._dr += rf
        return r

    def copy(self, out, in_):
        return self._emit([in_], [out], False, lambda: self.e.tensor_copy(out, in_))

    def tt(self, out, a, b, op):
        return self._emit([a, b], [out], False,
                          lambda: self.e.tensor_tensor(out, a, b, op=op))

    def memset(self, ap, v):
        return self._emit([], [ap], False, lambda: self.e.memset(ap, v))

    def iota(self, ap, pattern, base=0, cm=0):
        return self._emit([], [ap], True,
                          lambda: self.e.iota(ap, pattern, base=base,
                                              channel_multiplier=cm,
                                              allow_small_or_imprecise_dtypes=True))

    def ts(self, out, in0, s1, s2, op0, op1=ALU.add, accum=None):
        custom = not isinstance(s1, (int, float)) or accum is not None
        if s2 is None:
            s2 = 0.0
        reads = [in0] + ([s1] if not isinstance(s1, (int, float)) else [])
        writes = [out] + ([accum] if accum is not None else [])
        return self._emit(reads, writes, custom,
                          lambda: self.e.tensor_scalar(out=out, in0=in0, scalar1=s1,
                                                       scalar2=s2, op0=op0, op1=op1,
                                                       accum_out=accum))

    def stt(self, out, in0, s, in1, op0, op1, accum=None):
        reads = [in0, in1] + ([s] if not isinstance(s, (int, float)) else [])
        writes = [out] + ([accum] if accum is not None else [])
        return self._emit(reads, writes, True,
                          lambda: self.e.scalar_tensor_tensor(out=out, in0=in0, scalar=s,
                                                              in1=in1, op0=op0, op1=op1,
                                                              accum_out=accum))

    def reduce(self, out, in_, op):
        return self._emit([in_], [out], True,
                          lambda: self.e.tensor_reduce(out, in_, axis=AX.X, op=op))

    def recip(self, out, in_):
        return self._emit([in_], [out], True, lambda: self.e.reciprocal(out, in_))

    def scan(self, out, d0, d1, init, op0=ALU.mult, op1=ALU.add):
        reads = [d0, d1] + ([init] if not isinstance(init, (int, float)) else [])
        return self._emit(reads, [out], True,
                          lambda: self.e.tensor_tensor_scan(out=out, data0=d0, data1=d1,
                                                            initial=init, op0=op0,
                                                            op1=op1))

    def act(self, out, in_, func, bias=0.0, scale=1.0, accum=None, check=True):
        custom = accum is not None or not isinstance(bias, (int, float))
        reads = [in_] + ([bias] if not isinstance(bias, (int, float)) else [])
        writes = [out] + ([accum] if accum is not None else [])
        return self._emit(reads, writes, custom,
                          lambda: self.e.activation(out=out, in_=in_, func=func,
                                                    bias=bias, scale=scale,
                                                    accum_out=accum))

    def wait(self, sem, v):
        return self.e.wait_ge(sem, v)

    def inc(self, sem, v=1):
        return self.e.sem_inc(sem, v)

    def nop_inc(self, sem, v=1):
        self.e.drain()
        self._dw = []
        self._dr = []
        return self.e.nop().then_inc(sem, v)

    def drain(self):
        self.e.drain()
        self._dw = []
        self._dr = []


SEM_NAMES = ["pe", "ve", "ac", "gp", "ev", "bar"]


class Sems:
    def __init__(self, nc, ctx):
        self.h = {n: ctx.enter_context(nc.semaphore(n)) for n in SEM_NAMES}
        self.base = {n: 0 for n in SEM_NAMES}

    def __getitem__(self, n):
        return self.h[n]


class DmaRing:

    def __init__(self, nc, ctx, name, depth):
        self.sems = [ctx.enter_context(nc.semaphore(f"{name}{j}"))
                     for j in range(depth)]
        self.depth = depth
        self.n = 0

    def issue(self, raw, out, in_, **kw):
        j = self.n % self.depth
        k = self.n // self.depth
        if k > 0:
            raw.wait_ge(self.sems[j], 16 * k)
        raw.dma_start(out=out, in_=in_, **kw).then_inc(self.sems[j], 16)
        self.n += 1
        return self.n - 1

    def wait(self, raw, dma_id):
        if dma_id < 0:
            return
        j = dma_id % self.depth
        raw.wait_ge(self.sems[j], 16 * (dma_id // self.depth + 1))

    def wait_upto(self, raw, last_id):
        for j in range(min(self.depth, last_id + 1)):
            idj = last_id - ((last_id - j) % self.depth)
            raw.wait_ge(self.sems[j], 16 * (idj // self.depth + 1))

    def wait_all(self, raw):
        for j in range(self.depth):
            if self.n > j:
                cnt = (self.n - 1 - j) // self.depth + 1
                raw.wait_ge(self.sems[j], 16 * cnt)


def phase_ln(nc, S, R, TT, mode, bar_k=0):
    is_A = (mode == "A")
    name = "phA_" if is_A else "phE_"
    pctx = ExitStack()
    sb = lambda n, s, d=F32: pctx.enter_context(nc.sbuf_tensor(name + n, s, d))
    psal = lambda n, s, d=F32: pctx.enter_context(nc.psum_tensor(name + n, s, d))

    x0, fm = TT["x"], TT["fm"]

    xt = sb("xt", [128, 2, 2048])
    stat = sb("stat", [128, 8])
    sqs = sb("sqs", [128, 2048])
    fmt = sb("fmt", [128, 2, 16, 256], BF16)
    ident = sb("ident", [128, 128])
    io_a = sb("io_a", [128, 128])
    io_b = sb("io_b", [128, 128])
    ptrA = psal("ptrA", [128, 512])
    ptrB = psal("ptrB", [128, 512])
    epsb = sb("epsb", [128, 1])

    if is_A:
        racc = sb("racc", [128, 16])
        racct = sb("racct", [128, 2, 16])
        gwl = sb("gwl", [128, 96])
        r6 = sb("r6", [1, 48])
        rr = sb("rr", [1, 8])
        routef_s = sb("routef", [1, 8])
        routei_s = sb("routei", [1, 2], I32)
        iota6 = sb("iota6", [1, 6])
        pgate = psal("pgate", [1, 6])
    else:
        gwl = sb("gwl", [128, 128], BF16)
        combt = sb("combt", [128, 2, 8])
        combsc = sb("combsc", [128, 2, 8])
        mrow = sb("mrow", [128, 4])
        pscore = psal("pscore", [128, 2, 8])

    ve0 = S.base["ve"]; ac0 = S.base["ac"]
    pe0 = S.base["pe"]; gp0 = S.base["gp"]; evz = S.base["ev"]
    VE, AC, PE, GP, EV = S["ve"], S["ac"], S["pe"], S["gp"], S["ev"]
    ldR, wtR, stR, x2R = R["ld"], R["wt"], R["st"], R["x2"]
    ld0, wt0, st0, x20 = ldR.n, wtR.n, stR.n, x2R.n
    VPT = 3 if is_A else 5
    APT = 1 if is_A else 2
    PPT = 16 if is_A else 17
    BAR = S["bar"]
    barv = 5 * bar_k

    block = pctx.enter_context(nc.Block(name))

    @block.sync
    def _(raw):
        raw.wait_ge(BAR, barv)
        if is_A:
            ldR.issue(raw, gwl[:, :], TT["gw"])
        for i in range(NT):
            if i >= 2:
                raw.wait_ge(PE, pe0 + PPT * (i - 2) + 16)
                raw.wait_ge(AC, ac0 + APT * (i - 2) + 1)
            ldR.issue(raw, xt[:, i % 2, :], x0[128 * i:128 * (i + 1), :])
        raw.drain()
        raw.sem_inc(BAR, 1)

    @block.gpsimd
    def _(raw):
        raw.wait_ge(BAR, barv)
        e = EW(raw)
        e.iota(io_a[:, :], [[1, 128]], base=0, cm=0)
        e.iota(io_b[:, :], [[0, 128]], base=0, cm=1)
        if is_A:
            e.iota(iota6[:, :], [[1, 6]], base=0, cm=0)
        else:
            wtR.issue(raw, gwl[:, :], TT["gw"])
        e.memset(epsb[:, :], 1e-5).then_inc(GP, 1)
        raw.drain()
        raw.sem_inc(BAR, 1)

    @block.vector
    def _(raw):
        raw.wait_ge(BAR, barv)
        e = EW(raw)
        e.wait(GP, gp0 + 1)
        e.tt(ident[:, :], io_a[:, :], io_b[:, :], ALU.is_equal)
        if is_A:
            e.memset(racc[:, :], 0.0)
        ldsk = 1 if is_A else 0
        for i in range(NT):
            p = i % 2
            tile = xt[:, p, :]
            mu = stat[:, 4 * p + 0:4 * p + 1]
            rstd = stat[:, 4 * p + 2:4 * p + 3]
            sd = stat[:, 4 * p + 3:4 * p + 4]
            if is_A and i >= 2:
                e.wait(EV, evz + 16 * (i - 1))
                e.tt(racc[:, :], racc[:, :], racct[:, i % 2, :], ALU.add)
            e.nop_inc(VE, 1)
            ldR.wait(raw, ld0 + ldsk + i)
            e.reduce(mu, tile, ALU.add)
            e.ts(mu, mu, 1.0 / D, 0.0, ALU.mult)
            e.ts(tile, tile, mu, 0.0, ALU.subtract)
            e.nop_inc(VE, 1)
            e.wait(AC, ac0 + APT * i + 1)
            e.recip(rstd, sd)
            e.ts(tile, tile, rstd, 0.0, ALU.mult)
            e.nop_inc(VE, 1)
            if not is_A:
                e.wait(PE, pe0 + PPT * i + 17)
                sc_ps = pscore[:, i % 2, :]
                m1 = mrow[:, 0:1]; m2 = mrow[:, 1:2]; ssum = mrow[:, 2:3]
                lg = combsc[:, i % 2, :]
                e.reduce(m1, sc_ps, ALU.max)
                e.ts(lg, sc_ps, m1, 0.0, ALU.subtract)
                e.nop_inc(VE, 1)
                e.wait(AC, ac0 + 2 * i + 2)
                e.reduce(ssum, lg, ALU.add)
                e.recip(ssum, ssum)
                prb = combt[:, i % 2, :]
                if i >= 2:
                    x2R.wait(raw, x20 + i - 2)
                e.ts(prb, lg, ssum, 0.0, ALU.mult)
                e.reduce(m1, prb, ALU.max)
                e.ts(lg, prb, m1, 0.0, ALU.is_equal)
                e.stt(lg, lg, -2.0, prb, ALU.mult, ALU.add)
                e.reduce(m2, lg, ALU.max)
                e.ts(lg, prb, m2, 0.0, ALU.is_ge)
                e.tt(prb, prb, lg, ALU.mult)
                e.nop_inc(VE, 1)
        if is_A:
            e.wait(EV, evz + 16 * (NT - 1))
            e.tt(racc[:, :], racc[:, :], racct[:, (NT - 2) % 2, :], ALU.add)
            e.wait(EV, evz + 16 * NT)
            e.tt(racc[:, :], racc[:, :], racct[:, (NT - 1) % 2, :], ALU.add)
            e.nop_inc(VE, 1)
            e.wait(PE, pe0 + 16 * NT + 1)
            lgt = r6[0:1, 0:6]
            e.ts(lgt, pgate[0:1, :], 1.0 / T, 0.0, ALU.mult)
            m1 = rr[0:1, 0:1]
            e.reduce(m1, lgt, ALU.max)
            e.ts(rr[0:1, 1:2], m1, -1.0, 0.0, ALU.mult)
            e.nop_inc(VE, 1)
            e.wait(AC, ac0 + NT + 1)
            e6 = r6[0:1, 8:14]
            s1 = rr[0:1, 2:3]
            e.reduce(s1, e6, ALU.add)
            e.recip(s1, s1)
            probs = r6[0:1, 16:22]
            e.ts(probs, e6, s1, 0.0, ALU.mult)
            p1 = rr[0:1, 3:4]
            e.reduce(p1, probs, ALU.max)
            eq = r6[0:1, 24:30]
            e.ts(eq, probs, p1, 0.0, ALU.is_equal)
            t6 = r6[0:1, 32:38]
            e.tt(t6, eq, iota6[0:1, :], ALU.mult)
            e.reduce(routef_s[0:1, 0:1], t6, ALU.add)
            msk = r6[0:1, 40:46]
            e.stt(msk, eq, -2.0, probs, ALU.mult, ALU.add)
            p2 = rr[0:1, 4:5]
            e.reduce(p2, msk, ALU.max)
            eq2 = r6[0:1, 24:30]
            e.ts(eq2, probs, p2, 0.0, ALU.is_equal)
            e.tt(t6, eq2, iota6[0:1, :], ALU.mult)
            e.reduce(routef_s[0:1, 1:2], t6, ALU.add)
            ws = rr[0:1, 5:6]
            e.tt(ws, p1, p2, ALU.add)
            e.recip(ws, ws)
            e.ts(routef_s[0:1, 2:3], p1, ws, 0.0, ALU.mult)
            e.ts(routef_s[0:1, 3:4], p2, ws, 0.0, ALU.mult)
            e.copy(routef_s[0:1, 4:5], p1)
            e.copy(routef_s[0:1, 5:6], p2)
            e.memset(routef_s[0:1, 6:8], 0.0)
            e.drain()
            e.copy(routei_s[0:1, 0:2], routef_s[0:1, 0:2])
            e.nop_inc(VE, 1)
        raw.drain()
        raw.sem_inc(BAR, 1)

    @block.tensor
    def _(raw):
        raw.wait_ge(BAR, barv)
        for i in range(NT):
            raw.wait_ge(VE, ve0 + VPT * i + 3)
            p = i % 2
            for c in range(NCH):
                gidx = 16 * i + c
                if gidx >= 2:
                    raw.wait_ge(EV, evz + gidx - 1)
                dst = (ptrA if c % 2 == 0 else ptrB)
                raw.transpose(dst[:, 0:128], xt[:, p, 128 * c:128 * (c + 1)],
                              ident[:, :]).then_inc(PE, 1)
            if not is_A:
                if i == 0:
                    wtR.wait(raw, wt0)
                if i >= 2:
                    raw.wait_ge(VE, ve0 + 5 * (i - 2) + 5)
                bi, ti = i // 2, i % 2
                pb = bi % 2
                for c in range(NCH):
                    raw.wait_ge(EV, evz + 16 * i + c + 1)
                    mmi = raw.matmul(pscore[:, i % 2, :],
                                     fmt[:, pb, c, 128 * ti:128 * ti + 128],
                                     gwl[:, 8 * c:8 * (c + 1)],
                                     start=(c == 0), stop=(c == NCH - 1))
                mmi.then_inc(PE, 1)
        if is_A:
            raw.wait_ge(VE, ve0 + 3 * NT + 1)
            for c in range(NCH):
                mmi = raw.matmul(pgate[0:1, :], racc[:, c:c + 1],
                                 gwl[:, 6 * c:6 * (c + 1)],
                                 start=(c == 0), stop=(c == NCH - 1))
            mmi.then_inc(PE, 1)
        raw.drain()
        raw.sem_inc(BAR, 1)

    @block.scalar
    def _(raw):
        raw.wait_ge(BAR, barv)
        e = EW(raw)
        for i in range(NT):
            p = i % 2
            tile = xt[:, p, :]
            var = stat[:, 4 * p + 1:4 * p + 2]
            sd = stat[:, 4 * p + 3:4 * p + 4]
            e.wait(VE, ve0 + VPT * i + 2)
            e.act(sqs[:, :], tile, AF.Square, accum=var)
            if i == 0:
                e.wait(GP, gp0 + 1)
            e.act(sd, var, AF.Sqrt, bias=epsb[:, 0:1], scale=1.0 / D).then_inc(AC, 1)
            bi, ti = i // 2, i % 2
            pb = bi % 2
            if ti == 0 and bi >= 2:
                stR.wait(raw, st0 + bi - 2)
            if is_A:
                e.wait(VE, ve0 + 3 * i + 1)
            for c in range(NCH):
                e.wait(PE, pe0 + PPT * i + c + 1)
                src = (ptrA if c % 2 == 0 else ptrB)
                acc = racct[:, p, c:c + 1] if is_A else None
                e.act(fmt[:, pb, c, 128 * ti:128 * ti + 128], src[:, 0:128],
                      AF.Copy, accum=acc, check=False).then_inc(EV, 1)
            if ti == 1:
                raw.wait_ge(EV, evz + 16 * (i + 1))
                stR.issue(raw, fm[bi].rearrange("c d t -> d c t"), fmt[:, pb, :, :])
            if not is_A:
                e.wait(VE, ve0 + 5 * i + 4)
                e.act(combsc[:, i % 2, :], combsc[:, i % 2, :],
                      AF.Exp).then_inc(AC, 1)
                e.wait(VE, ve0 + 5 * i + 5)
                x2R.issue(raw, TT["comb"][128 * i:128 * (i + 1), :],
                          combt[:, i % 2, :])
        if is_A:
            e.wait(VE, ve0 + 3 * NT + 2)
            e.act(r6[0:1, 8:14], r6[0:1, 0:6], AF.Exp,
                  bias=rr[0:1, 1:2]).then_inc(AC, 1)
            e.wait(VE, ve0 + 3 * NT + 3)
            x2R.issue(raw, TT["route_f"], routef_s[0:1, :])
            x2R.issue(raw, TT["route_i"], routei_s[0:1, :])
        x2R.wait_all(raw)
        stR.wait_all(raw)
        raw.drain()
        raw.sem_inc(BAR, 1)

    pctx.close()

    S.base["gp"] += 1
    S.base["pe"] += (16 * NT + 1) if is_A else 17 * NT
    S.base["ev"] += 16 * NT
    S.base["ve"] += (3 * NT + 3) if is_A else 5 * NT
    S.base["ac"] += (NT + 1) if is_A else 2 * NT


def phase_B(nc, S, R, TT, bar_k=0):
    name = "phB_"
    pctx = ExitStack()
    sb = lambda n, s, d=F32: pctx.enter_context(nc.sbuf_tensor(name + n, s, d))
    psal = lambda n, s, d=F32: pctx.enter_context(nc.psum_tensor(name + n, s, d))

    xn, a0f = TT["xn"], TT["a0f"]

    wq_s = sb("wq", [128, 16, 2048], BF16)
    wv_s = sb("wv", [128, 16, 2048], BF16)
    xnb = sb("xnb", [128, 2, 16, 256], BF16)
    vfm = sb("vfm", [128, 16, 256], BF16)
    retb = sb("retb", [128, 16, 256])
    sqb = sb("sqb", [128, 2, 256])
    t1b = sb("t1b", [128, 256])
    a0b = sb("a0b", [128, 16, 256], BF16)
    lam_col = sb("lam", [128, 16])
    state = sb("state", [128, 16])
    gam = sb("gam", [128, 16])
    bet = sb("bet", [128, 16])
    ones_c = sb("ones_c", [128, 1])
    ones_r = sb("ones_r", [1, 128])
    brow = sb("brow", [1, 16])
    lrow = sb("lrow", [1, 16])
    murow = sb("murow", [1, 256])
    varrow = sb("varrow", [1, 256])
    sdrow = sb("sdrow", [1, 256])
    rstdrow = sb("rstdrow", [1, 256])
    epsb = sb("epsb", [1, 1])

    pvA = psal("pvA", [128, 256]); pvB = psal("pvB", [128, 256])
    pqA = psal("pqA", [128, 256]); pqB = psal("pqB", [128, 256])
    psu = psal("psu", [1, 256]); psq = psal("psq", [1, 256])
    pmu = psal("pmu", [128, 256]); prs = psal("prs", [128, 256])

    ve0 = S.base["ve"]; ac0 = S.base["ac"]; pe0 = S.base["pe"]
    gp0 = S.base["gp"]; evz = S.base["ev"]
    VE, AC, PE, GP, EV = S["ve"], S["ac"], S["pe"], S["gp"], S["ev"]
    ldR, wtR, stR = R["ld"], R["wt"], R["st"]
    ld0, wt0, st0 = ldR.n, wtR.n, stR.n
    BAR = S["bar"]
    barv = 5 * bar_k
    PEs, VEs, ACs = pe0 + 1, ve0 + 2, ac0 + 1

    def pe_at(bi, k):
        return PEs + 50 * bi + k

    def ve_at(bi, k):
        return VEs + 19 * bi + k

    def ac_at(bi, k):
        return ACs + 17 * bi + k

    block = pctx.enter_context(nc.Block(name))

    @block.sync
    def _(raw):
        raw.wait_ge(BAR, barv)
        ldR.issue(raw, brow[:, :], TT["beta0"])
        ldR.issue(raw, gam[:, :], TT["gamma_l"])
        ldR.issue(raw, bet[:, :], TT["beta_l"])
        for bi in range(NB):
            if bi >= 2:
                raw.wait_ge(PE, pe_at(bi - 2, 32))
            ldR.issue(raw, xnb[:, bi % 2, :, :],
                      xn[bi].rearrange("c d t -> d c t"))
        raw.drain()
        raw.sem_inc(BAR, 1)

    @block.gpsimd
    def _(raw):
        raw.wait_ge(BAR, barv)
        e = EW(raw)
        e.memset(ones_c[:, :], 1.0)
        e.memset(ones_r[:, :], 1.0)
        e.memset(state[:, :], 0.0)
        e.memset(epsb[:, :], 1e-5).then_inc(GP, 1)
        for k in range(NCH):
            wtR.issue(raw, wv_s[:, k, :], TT["wvT"][128 * k:128 * (k + 1), :])
        for k in range(NCH):
            wtR.issue(raw, wq_s[:, k, :], TT["wqT"][128 * k:128 * (k + 1), :])
        raw.drain()
        raw.sem_inc(BAR, 1)

    @block.scalar
    def _(raw):
        raw.wait_ge(BAR, barv)
        e = EW(raw)
        ldR.wait(raw, ld0)
        e.act(lrow[:, :], brow[:, :], AF.Exp, scale=-LN2).then_inc(AC, 1)
        for bi in range(NB):
            pb = bi % 2
            if bi >= 1:
                e.wait(VE, ve_at(bi - 1, 1))
            for c in range(NCH):
                e.wait(PE, pe_at(bi, c + 1))
                e.act(vfm[:, c, :], (pvA if c % 2 == 0 else pvB)[:, :],
                      AF.Copy).then_inc(EV, 1)
            for c in range(NCH):
                e.wait(VE, ve_at(bi, c + 1))
                if c >= 2:
                    e.wait(PE, pe_at(bi, 33 + c - 1))
                e.act(sqb[:, c % 2, :], retb[:, c, :], AF.Square).then_inc(AC, 1)
            e.wait(VE, ve_at(bi, 17))
            e.act(sdrow[:, :], varrow[:, :], AF.Sqrt,
                  bias=epsb[0:1, 0:1]).then_inc(AC, 1)
            e.wait(VE, ve_at(bi, 19))
            stR.issue(raw, a0f[bi].rearrange("c d t -> d c t"), a0b[:, :, :])
        stR.wait_all(raw)
        raw.drain()
        raw.sem_inc(BAR, 1)

    @block.tensor
    def _(raw):
        raw.wait_ge(BAR, barv)
        raw.wait_ge(VE, VEs - 1)
        raw.wait_ge(GP, gp0 + 1)
        raw.matmul(pmu[:, 0:16], ones_r[0:1, :], lrow[0:1, :],
                   start=True, stop=True).then_inc(PE, 1)
        for bi in range(NB):
            pb = bi % 2
            for c in range(NCH):
                if 16 * bi + c >= 2:
                    raw.wait_ge(EV, evz + 16 * bi + c - 1)
                dst = (pvA if c % 2 == 0 else pvB)
                if c == 0:
                    ldR.wait(raw, ld0 + 3 + bi)
                for k in range(NCH):
                    if bi == 0 and c == 0:
                        wtR.wait(raw, wt0 + k)
                    mmi = raw.matmul(dst[:, :], wv_s[:, k, 128 * c:128 * (c + 1)],
                                     xnb[:, pb, k, :], start=(k == 0),
                                     stop=(k == NCH - 1))
                mmi.then_inc(PE, 1)
            for c in range(NCH):
                if c >= 2:
                    raw.wait_ge(VE, ve_at(bi, c - 1))
                elif bi >= 1:
                    raw.wait_ge(VE, ve_at(bi - 1, 14 + c + 1))
                dst = (pqA if c % 2 == 0 else pqB)
                for k in range(NCH):
                    if bi == 0 and c == 0:
                        wtR.wait(raw, wt0 + 16 + k)
                    mmi = raw.matmul(dst[:, :], wq_s[:, k, 128 * c:128 * (c + 1)],
                                     xnb[:, pb, k, :], start=(k == 0),
                                     stop=(k == NCH - 1))
                mmi.then_inc(PE, 1)
            if bi >= 1:
                raw.wait_ge(VE, ve_at(bi - 1, 17))
            for c in range(NCH):
                raw.wait_ge(VE, ve_at(bi, c + 1))
                mmi = raw.matmul(psu[0:1, :], ones_c[:, 0:1], retb[:, c, :],
                                 start=(c == 0), stop=(c == NCH - 1))
            mmi.then_inc(PE, 1)
            for c in range(NCH):
                raw.wait_ge(AC, ac_at(bi, c + 1))
                mmi = raw.matmul(psq[0:1, :], ones_c[:, 0:1], sqb[:, c % 2, :],
                                 start=(c == 0), stop=(c == NCH - 1))
                mmi.then_inc(PE, 1)
            raw.wait_ge(VE, ve_at(bi, 18))
            if bi >= 1:
                raw.wait_ge(VE, ve_at(bi - 1, 19))
            else:
                raw.wait_ge(VE, VEs)
            raw.matmul(pmu[:, :], ones_r[0:1, :], murow[0:1, :],
                       start=True, stop=True)
            mmi = raw.matmul(prs[:, :], ones_r[0:1, :], rstdrow[0:1, :],
                             start=True, stop=True)
            mmi.then_inc(PE, 1)
        raw.drain()
        raw.sem_inc(BAR, 1)

    @block.vector
    def _(raw):
        raw.wait_ge(BAR, barv)
        e = EW(raw)
        e.wait(AC, ACs)
        e.ts(lrow[:, :], lrow[:, :], -1.0, 1.0, ALU.mult, ALU.add)
        e.ts(lrow[:, :], lrow[:, :], TINY, None, ALU.max)
        e.ts(lrow[:, :], lrow[:, :], 1.0 - 1e-9, None, ALU.min)
        e.nop_inc(VE, 1)
        e.wait(PE, PEs)
        e.copy(lam_col[:, :], pmu[:, 0:16])
        ldR.wait(raw, ld0 + 2)
        e.nop_inc(VE, 1)
        for bi in range(NB):
            for h in range(H):
                e.wait(EV, evz + 16 * bi + h + 1)
                if bi >= 1 and h == 0:
                    e.wait(AC, ac_at(bi - 1, 16))
                    e.wait(PE, pe_at(bi - 1, 33))
                lam_b = bass.AP(lam_col.tensor if hasattr(lam_col, "tensor")
                                else lam_col, h, [[16, 128], [0, TB]])
                e.scan(retb[:, h, :], lam_b, vfm[:, h, :], state[:, h:h + 1])
            e.copy(state[:, :], retb[:, :, TB - 1:TB])
            for c in range(NCH):
                e.wait(PE, pe_at(bi, 16 + c + 1))
                src = (pqA if c % 2 == 0 else pqB)
                e.tt(retb[:, c, :], src[:, :], retb[:, c, :],
                     ALU.mult).then_inc(VE, 1)
            e.wait(PE, pe_at(bi, 33))
            e.ts(murow[:, :], psu[0:1, :], 1.0 / D, 0.0, ALU.mult)
            e.wait(PE, pe_at(bi, 49))
            e.ts(varrow[:, :], psq[0:1, :], 1.0 / D, 0.0, ALU.mult)
            e.tt(sdrow[:, :], murow[:, :], murow[:, :], ALU.mult)
            e.tt(varrow[:, :], varrow[:, :], sdrow[:, :], ALU.subtract)
            e.nop_inc(VE, 1)
            e.wait(AC, ac_at(bi, 17))
            e.recip(rstdrow[:, :], sdrow[:, :])
            e.nop_inc(VE, 1)
            e.wait(PE, pe_at(bi, 50))
            if bi >= 1:
                stR.wait(raw, st0 + bi - 1)
            for c in range(NCH):
                e.tt(t1b[:, :], retb[:, c, :], pmu[:, :], ALU.subtract)
                e.tt(t1b[:, :], t1b[:, :], prs[:, :], ALU.mult)
                e.ts(a0b[:, c, :], t1b[:, :], gam[:, c:c + 1], bet[:, c:c + 1],
                     ALU.mult, ALU.add)
            e.nop_inc(VE, 1)
        raw.drain()
        raw.sem_inc(BAR, 1)

    pctx.close()

    S.base["gp"] += 1
    S.base["pe"] += 1 + 50 * NB
    S.base["ve"] += 2 + 19 * NB
    S.base["ac"] += 1 + 17 * NB
    S.base["ev"] += 16 * NB


def phase_C(nc, S, R, TT, bar_k=0):
    name = "phC_"
    pctx = ExitStack()
    sb = lambda n, s, d=F32: pctx.enter_context(nc.sbuf_tensor(name + n, s, d))
    psal = lambda n, s, d=F32: pctx.enter_context(nc.psum_tensor(name + n, s, d))

    xn, a0f, x0, acc0 = TT["xn"], TT["a0f"], TT["x0"], TT["acc0"]

    wg_s = sb("wg", [128, 16, 2048], BF16)
    wo_s = sb("wo", [128, 16, 2048], BF16)
    xnb = sb("xnb", [128, 2, 16, 256], BF16)
    a0s = sb("a0s", [128, 16, 256], BF16)
    a1b = sb("a1b", [128, 16, 256], BF16)
    gsb = sb("gsb", [128, 2, 256])
    acct = sb("acct", [128, 2, 2048])
    x0t = sb("x0t", [128, 2, 2048])

    pgA = psal("pgA", [128, 256]); pgB = psal("pgB", [128, 256])
    pat = psal("pat", [128, 2048])

    ve0 = S.base["ve"]; pe0 = S.base["pe"]; evz = S.base["ev"]
    VE, PE, EV = S["ve"], S["pe"], S["ev"]
    ldR, wtR, stR = R["ld"], R["wt"], R["st"]
    ld0, wt0, st0 = ldR.n, wtR.n, stR.n
    BAR = S["bar"]
    barv = 5 * bar_k

    def pe_at(bi, k):
        return pe0 + 18 * bi + k

    def ve_at(bi, k):
        return ve0 + 18 * bi + k

    block = pctx.enter_context(nc.Block(name))

    @block.sync
    def _(raw):
        raw.wait_ge(BAR, barv)
        for bi in range(NB):
            if bi >= 2:
                raw.wait_ge(PE, pe_at(bi - 2, 16))
            ldR.issue(raw, xnb[:, bi % 2, :, :], xn[bi].rearrange("c d t -> d c t"))
            if bi >= 1:
                raw.wait_ge(VE, ve_at(bi - 1, 16))
            ldR.issue(raw, a0s[:, :, :], a0f[bi].rearrange("c d t -> d c t"))
            for ti in range(2):
                j = 2 * bi + ti
                if j >= 2:
                    raw.wait_ge(VE, ve_at((j - 2) // 2, 17 + (j - 2) % 2))
                ldR.issue(raw, x0t[:, j % 2, :],
                          x0[128 * j:128 * (j + 1), :])
        raw.drain()
        raw.sem_inc(BAR, 1)

    @block.gpsimd
    def _(raw):
        raw.wait_ge(BAR, barv)
        for k in range(NCH):
            wtR.issue(raw, wg_s[:, k, :], TT["wgT"][128 * k:128 * (k + 1), :])
        for k in range(NCH):
            wtR.issue(raw, wo_s[:, k, :], TT["woT"][128 * k:128 * (k + 1), :])
        raw.drain()
        raw.sem_inc(BAR, 1)

    @block.tensor
    def _(raw):
        raw.wait_ge(BAR, barv)
        for bi in range(NB):
            pb = bi % 2
            for c in range(NCH):
                if c == 0:
                    ldR.wait(raw, ld0 + 4 * bi)
                if 16 * bi + c >= 2:
                    raw.wait_ge(EV, evz + 16 * bi + c - 1)
                dst = (pgA if c % 2 == 0 else pgB)
                for k in range(NCH):
                    if bi == 0 and c == 0:
                        wtR.wait(raw, wt0 + k)
                    mmi = raw.matmul(dst[:, :], wg_s[:, k, 128 * c:128 * (c + 1)],
                                     xnb[:, pb, k, :], start=(k == 0),
                                     stop=(k == NCH - 1))
                mmi.then_inc(PE, 1)
            for ti in range(2):
                j = 2 * bi + ti
                if j >= 1:
                    raw.wait_ge(VE, ve_at((j - 1) // 2, 17 + (j - 1) % 2))
                for c in range(NCH):
                    raw.wait_ge(VE, ve_at(bi, c + 1))
                    if bi == 0 and c == 0:
                        wtR.wait_upto(raw, wt0 + 31)
                    for jj in range(4):
                        mmi = raw.matmul(pat[:, 512 * jj:512 * (jj + 1)],
                                         a1b[:, c, 128 * ti:128 * (ti + 1)],
                                         wo_s[:, c, 512 * jj:512 * (jj + 1)],
                                         start=(c == 0), stop=(c == NCH - 1))
                mmi.then_inc(PE, 1)
        raw.drain()
        raw.sem_inc(BAR, 1)

    @block.scalar
    def _(raw):
        raw.wait_ge(BAR, barv)
        e = EW(raw)
        for bi in range(NB):
            for c in range(NCH):
                e.wait(PE, pe_at(bi, c + 1))
                if c >= 2:
                    e.wait(VE, ve_at(bi, c - 1))
                e.act(gsb[:, c % 2, :], (pgA if c % 2 == 0 else pgB)[:, :],
                      AF.Silu).then_inc(EV, 1)
            for ti in range(2):
                e.wait(VE, ve_at(bi, 17 + ti))
                stR.issue(raw, acc0[128 * (2 * bi + ti):128 * (2 * bi + ti + 1), :],
                          acct[:, (2 * bi + ti) % 2, :])
        stR.wait_all(raw)
        raw.drain()
        raw.sem_inc(BAR, 1)

    @block.vector
    def _(raw):
        raw.wait_ge(BAR, barv)
        e = EW(raw)
        for bi in range(NB):
            for c in range(NCH):
                e.wait(EV, evz + 16 * bi + c + 1)
                if c == 0:
                    ldR.wait(raw, ld0 + 4 * bi + 1)
                    if bi >= 1:
                        e.wait(PE, pe_at(bi - 1, 18))
                e.tt(a1b[:, c, :], gsb[:, c % 2, :], a0s[:, c, :],
                     ALU.mult).then_inc(VE, 1)
            for ti in range(2):
                j = 2 * bi + ti
                e.wait(PE, pe_at(bi, 17 + ti))
                ldR.wait(raw, ld0 + 4 * bi + 2 + ti)
                if j >= 2:
                    stR.wait(raw, st0 + j - 2)
                for jj in range(4):
                    e.tt(acct[:, j % 2, 512 * jj:512 * (jj + 1)],
                         pat[:, 512 * jj:512 * (jj + 1)],
                         x0t[:, j % 2, 512 * jj:512 * (jj + 1)], ALU.add)
                e.nop_inc(VE, 1)
        raw.drain()
        raw.sem_inc(BAR, 1)

    pctx.close()

    S.base["pe"] += 18 * NB
    S.base["ve"] += 18 * NB
    S.base["ev"] += 16 * NB


def phase_S(nc, S, R, TT, slot, bar_k=0):
    name = f"phS{slot}_"
    pctx = ExitStack()
    sb = lambda n, s, d=F32: pctx.enter_context(nc.sbuf_tensor(name + n, s, d))
    psal = lambda n, s, d=F32: pctx.enter_context(nc.psum_tensor(name + n, s, d))

    xn, acc_in, acc_out = TT["xn"], TT["acc_in"], TT["acc_out"]
    NF = 11

    w1_s = sb("w1", [128, 16, 1408], BF16)
    w2_s = sb("w2", [128, 11, 2048], BF16)
    b1row = sb("b1row", [1, 1408], BF16)
    b2row = sb("b2row", [1, 2048], BF16)
    xnb = sb("xnb", [128, 2, 16, 256], BF16)
    hfm = sb("hfm", [128, 11, 256], BF16)
    acct = sb("acct", [128, 2, 2048])
    ri_s = sb("ri", [1, 2], I32)
    rf_s = sb("rf", [1, 8])
    ones_r = sb("ones_r", [1, 128])
    ones_rb = sb("ones_rb", [1, 128], BF16)
    ones256 = sb("ones256", [1, 256], BF16)
    wbc = sb("wbc", [128, 1])

    phA_ = psal("phA", [128, 256]); phB_ = psal("phB", [128, 256])
    peo = psal("peo", [128, 2048])
    pwb = psal("pwb", [128, 1])

    ve0 = S.base["ve"]; pe0 = S.base["pe"]; evz = S.base["ev"]; gp0 = S.base["gp"]
    VE, PE, EV, GP = S["ve"], S["pe"], S["ev"], S["gp"]
    ldR, wtR, stR = R["ld"], R["wt"], R["st"]
    ld0, wt0, st0 = ldR.n, wtR.n, stR.n
    BAR = S["bar"]
    barv = 5 * bar_k
    PEs, VEs = pe0 + 1, ve0 + 1

    def pe_at(bi, k):
        return PEs + 13 * bi + k

    def ve_at(bi, k):
        return VEs + 2 * bi + k

    block = pctx.enter_context(nc.Block(name))

    @block.sync
    def _(raw):
        raw.wait_ge(BAR, barv)
        ldR.issue(raw, ri_s[:, :], TT["route_i"])
        ldR.issue(raw, rf_s[:, :], TT["route_f"])
        for bi in range(NB):
            if bi >= 2:
                raw.wait_ge(PE, pe_at(bi - 2, 11))
            ldR.issue(raw, xnb[:, bi % 2, :, :], xn[bi].rearrange("c d t -> d c t"))
            for ti in range(2):
                j = 2 * bi + ti
                if j >= 2:
                    raw.wait_ge(VE, ve_at((j - 2) // 2, 1 + (j - 2) % 2))
                    stR.wait(raw, st0 + j - 2)
                ldR.issue(raw, acct[:, j % 2, :], acc_in[128 * j:128 * (j + 1), :])
        raw.drain()
        raw.sem_inc(BAR, 1)

    @block.gpsimd
    def _(raw):
        raw.wait_ge(BAR, barv)
        e = EW(raw)
        e.memset(ones_r[:, :], 1.0)
        e.memset(ones_rb[:, :], 1.0)
        e.memset(ones256[:, :], 1.0).then_inc(GP, 1)
        ldR.wait(raw, ld0)
        with raw.register(f"re{slot}") as re, raw.register(f"rw{slot}") as rw:
            raw.reg_load(re, ri_s[0:1, slot:slot + 1])
            raw.reg_mul(rw, re, 2048)
            rv = RuntimeValue(rw, min_val=0, max_val=5 * 2048)
            W1v = TT["W1"][ds(rv, 2048), :].rearrange("(k p) f -> p k f", p=128)
            wtR.issue(raw, w1_s[:, 0:8, :], W1v[:, 0:8, :])
            wtR.issue(raw, w1_s[:, 8:16, :], W1v[:, 8:16, :])
            rv = RuntimeValue(re, min_val=0, max_val=5)
            wtR.issue(raw, b1row[:, :], TT["b1"][ds(rv, 1), :])
            raw.reg_mul(rw, re, 1408)
            rv = RuntimeValue(rw, min_val=0, max_val=5 * 1408)
            W2v = TT["W2"][ds(rv, 1408), :].rearrange("(k p) f -> p k f", p=128)
            wtR.issue(raw, w2_s[:, 0:6, :], W2v[:, 0:6, :])
            wtR.issue(raw, w2_s[:, 6:11, :], W2v[:, 6:11, :])
            rv = RuntimeValue(re, min_val=0, max_val=5)
            wtR.issue(raw, b2row[:, 0:1024], TT["b2"][ds(rv, 1), 0:1024])
            wtR.issue(raw, b2row[:, 1024:2048], TT["b2"][ds(rv, 1), 1024:2048])
        raw.drain()
        raw.sem_inc(BAR, 1)

    @block.tensor
    def _(raw):
        raw.wait_ge(BAR, barv)
        ldR.wait(raw, ld0 + 1)
        raw.wait_ge(GP, gp0 + 1)
        raw.matmul(pwb[:, :], ones_r[0:1, :], rf_s[0:1, 2 + slot:3 + slot],
                   start=True, stop=True).then_inc(PE, 1)
        for bi in range(NB):
            pb = bi % 2
            for fc in range(NF):
                if fc == 0:
                    ldR.wait(raw, ld0 + 2 + 3 * bi)
                gidx = 11 * bi + fc
                if gidx >= 2:
                    raw.wait_ge(EV, evz + gidx - 1)
                dst = (phA_ if fc % 2 == 0 else phB_)
                if bi == 0 and fc == 0:
                    wtR.wait_upto(raw, wt0 + 6)
                raw.matmul(dst[:, :], b1row[0:1, 128 * fc:128 * (fc + 1)],
                           ones256[0:1, :], start=True, stop=False)
                for k in range(NCH):
                    mmi = raw.matmul(dst[:, :], w1_s[:, k, 128 * fc:128 * (fc + 1)],
                                     xnb[:, pb, k, :], start=False,
                                     stop=(k == NCH - 1))
                mmi.then_inc(PE, 1)
            for ti in range(2):
                j = 2 * bi + ti
                if j >= 1:
                    raw.wait_ge(VE, ve_at((j - 1) // 2, 1 + (j - 1) % 2))
                for jj in range(4):
                    raw.matmul(peo[:, 512 * jj:512 * (jj + 1)], ones_rb[0:1, :],
                               b2row[0:1, 512 * jj:512 * (jj + 1)],
                               start=True, stop=False)
                for fc in range(NF):
                    raw.wait_ge(EV, evz + 11 * bi + fc + 1)
                    for jj in range(4):
                        mmi = raw.matmul(peo[:, 512 * jj:512 * (jj + 1)],
                                         hfm[:, fc, 128 * ti:128 * (ti + 1)],
                                         w2_s[:, fc, 512 * jj:512 * (jj + 1)],
                                         start=False, stop=(fc == NF - 1))
                mmi.then_inc(PE, 1)
        raw.drain()
        raw.sem_inc(BAR, 1)

    @block.scalar
    def _(raw):
        raw.wait_ge(BAR, barv)
        e = EW(raw)
        for bi in range(NB):
            for fc in range(NF):
                e.wait(PE, pe_at(bi, fc + 1))
                if bi >= 1 and fc == 0:
                    e.wait(PE, pe_at(bi - 1, 13))
                e.act(hfm[:, fc, :], (phA_ if fc % 2 == 0 else phB_)[:, :],
                      AF.Gelu).then_inc(EV, 1)
            for ti in range(2):
                j = 2 * bi + ti
                e.wait(VE, ve_at(bi, 1 + ti))
                stR.issue(raw, acc_out[128 * j:128 * (j + 1), :],
                          acct[:, j % 2, :])
        stR.wait_all(raw)
        raw.drain()
        raw.sem_inc(BAR, 1)

    @block.vector
    def _(raw):
        raw.wait_ge(BAR, barv)
        e = EW(raw)
        e.wait(PE, PEs)
        e.copy(wbc[:, :], pwb[:, :])
        e.nop_inc(VE, 1)
        for bi in range(NB):
            for ti in range(2):
                j = 2 * bi + ti
                e.wait(PE, pe_at(bi, 12 + ti))
                ldR.wait(raw, ld0 + 2 + 3 * bi + 1 + ti)
                if j >= 2:
                    stR.wait(raw, st0 + j - 2)
                for jj in range(4):
                    e.stt(acct[:, j % 2, 512 * jj:512 * (jj + 1)],
                          peo[:, 512 * jj:512 * (jj + 1)], wbc[:, 0:1],
                          acct[:, j % 2, 512 * jj:512 * (jj + 1)],
                          ALU.mult, ALU.add)
                e.nop_inc(VE, 1)
        raw.drain()
        raw.sem_inc(BAR, 1)

    pctx.close()

    S.base["gp"] += 1
    S.base["pe"] += 1 + 13 * NB
    S.base["ve"] += 1 + 2 * NB
    S.base["ev"] += 11 * NB


def phase_F1(nc, S, R, TT, bar_k=0):
    name = "phF1_"
    pctx = ExitStack()
    sb = lambda n, s, d=F32: pctx.enter_context(nc.sbuf_tensor(name + n, s, d))
    psal = lambda n, s, d=F32: pctx.enter_context(nc.psum_tensor(name + n, s, d))

    xn, rkvf, rksf = TT["xn"], TT["rkvf"], TT["rksf"]

    wk_s = sb("wk", [128, 16, 2048], BF16)
    wv_s = sb("wv", [128, 16, 2048], BF16)
    xnb = sb("xnb", [128, 2, 16, 256], BF16)
    relb = sb("relb", [128, 2, 256])
    mnb = sb("mnb", [128, 2, 256])
    enb = sb("enb", [128, 2, 256])
    phikb = sb("phikb", [128, 16, 256], BF16)
    kvb = sb("kvb", [128, 16, 256], BF16)
    rkvb = sb("rkvb", [128, 16, 256])
    rksb = sb("rksb", [128, 16, 256])
    lam_col = sb("lam", [128, 16])
    state2 = sb("state2", [128, 32])
    drow = sb("drow", [1, 16])
    ones_r = sb("ones_r", [1, 128])

    pkA = psal("pkA", [128, 256]); pkB = psal("pkB", [128, 256])
    pvA = psal("pvA", [128, 256]); pvB = psal("pvB", [128, 256])
    plam = psal("plam", [128, 16])

    ve0 = S.base["ve"]; ac0 = S.base["ac"]; pe0 = S.base["pe"]
    gp0 = S.base["gp"]; evz = S.base["ev"]
    VE, AC, PE, GP, EV = S["ve"], S["ac"], S["pe"], S["gp"], S["ev"]
    ldR, wtR, stR = R["ld"], R["wt"], R["st"]
    ld0, wt0, st0 = ldR.n, wtR.n, stR.n
    BAR = S["bar"]
    barv = 5 * bar_k
    PEs, VEs, ACs = pe0 + 1, ve0 + 2, ac0 + 1

    def pe_at(bi, k):
        return PEs + 32 * bi + k

    def ve_at(bi, k):
        return VEs + 33 * bi + k

    def ac_at(bi, k):
        return ACs + 32 * bi + k

    block = pctx.enter_context(nc.Block(name))

    @block.sync
    def _(raw):
        raw.wait_ge(BAR, barv)
        ldR.issue(raw, drow[:, :], TT["decay1"])
        for bi in range(NB):
            if bi >= 2:
                raw.wait_ge(PE, pe_at(bi - 2, 32))
            ldR.issue(raw, xnb[:, bi % 2, :, :], xn[bi].rearrange("c d t -> d c t"))
        raw.drain()
        raw.sem_inc(BAR, 1)

    @block.gpsimd
    def _(raw):
        raw.wait_ge(BAR, barv)
        e = EW(raw)
        e.memset(ones_r[:, :], 1.0)
        e.memset(state2[:, :], 0.0).then_inc(GP, 1)
        for k in range(NCH):
            wtR.issue(raw, wk_s[:, k, :],
                      TT["w_qkv"][128 * k:128 * (k + 1), 2048:4096])
        for k in range(NCH):
            wtR.issue(raw, wv_s[:, k, :],
                      TT["w_qkv"][128 * k:128 * (k + 1), 4096:6144])
        for bi in range(NB):
            raw.wait_ge(VE, ve_at(bi, 33))
            wtR.issue(raw, rkvf[bi].rearrange("c d t -> d c t"), rkvb[:, :, :])
            wtR.issue(raw, rksf[bi].rearrange("c d t -> d c t"), rksb[:, :, :])
        wtR.wait_all(raw)
        raw.drain()
        raw.sem_inc(BAR, 1)

    @block.scalar
    def _(raw):
        raw.wait_ge(BAR, barv)
        e = EW(raw)
        ldR.wait(raw, ld0)
        e.act(drow[:, :], drow[:, :], AF.Sigmoid).then_inc(AC, 1)
        for bi in range(NB):
            for c in range(NCH):
                p = c % 2
                e.wait(PE, pe_at(bi, 2 * c + 1))
                e.act(relb[:, p, :], (pkA if p == 0 else pkB)[:, :],
                      AF.Relu).then_inc(AC, 1)
                e.wait(VE, ve_at(bi, 2 * c + 1))
                e.act(enb[:, p, :], mnb[:, p, :], AF.Exp).then_inc(AC, 1)
        raw.drain()
        raw.sem_inc(BAR, 1)

    @block.tensor
    def _(raw):
        raw.wait_ge(BAR, barv)
        raw.wait_ge(AC, ACs)
        raw.wait_ge(GP, gp0 + 1)
        raw.matmul(plam[:, :], ones_r[0:1, :], drow[0:1, :],
                   start=True, stop=True).then_inc(PE, 1)
        for bi in range(NB):
            pb = bi % 2
            for c in range(NCH):
                if c == 0:
                    ldR.wait(raw, ld0 + 1 + bi)
                p = c % 2
                if 16 * bi + c >= 2:
                    raw.wait_ge(AC, ac_at(bi, 2 * (c - 2) + 1))
                    raw.wait_ge(VE, ve_at(bi, 2 * (c - 2) + 1))
                dstk = (pkA if p == 0 else pkB)
                for k in range(NCH):
                    if bi == 0 and c == 0:
                        wtR.wait(raw, wt0 + k)
                    mmi = raw.matmul(dstk[:, :], wk_s[:, k, 128 * c:128 * (c + 1)],
                                     xnb[:, pb, k, :], start=(k == 0),
                                     stop=(k == NCH - 1))
                mmi.then_inc(PE, 1)
                if 16 * bi + c >= 2:
                    raw.wait_ge(VE, ve_at(bi, 2 * (c - 2) + 2))
                dstv = (pvA if p == 0 else pvB)
                for k in range(NCH):
                    if bi == 0 and c == 0:
                        wtR.wait(raw, wt0 + 16 + k)
                    mmi = raw.matmul(dstv[:, :], wv_s[:, k, 128 * c:128 * (c + 1)],
                                     xnb[:, pb, k, :], start=(k == 0),
                                     stop=(k == NCH - 1))
                mmi.then_inc(PE, 1)
        raw.drain()
        raw.sem_inc(BAR, 1)

    @block.vector
    def _(raw):
        raw.wait_ge(BAR, barv)
        e = EW(raw)
        e.wait(PE, PEs)
        e.copy(lam_col[:, :], plam[:, :])
        e.nop_inc(VE, 1)
        e.nop_inc(VE, 1)
        for bi in range(NB):
            if bi >= 1:
                wtR.wait(raw, wt0 + 32 + 2 * (bi - 1))
                wtR.wait(raw, wt0 + 32 + 2 * (bi - 1) + 1)
            for c in range(NCH):
                p = c % 2
                e.wait(PE, pe_at(bi, 2 * c + 1))
                e.wait(AC, ac_at(bi, 2 * c + 1))
                e.tt(mnb[:, p, :], (pkA if p == 0 else pkB)[:, :],
                     relb[:, p, :], ALU.subtract).then_inc(VE, 1)
                e.wait(AC, ac_at(bi, 2 * c + 2))
                e.tt(phikb[:, c, :], enb[:, p, :], relb[:, p, :], ALU.add)
                e.wait(PE, pe_at(bi, 2 * c + 2))
                e.tt(kvb[:, c, :], (pvA if p == 0 else pvB)[:, :],
                     phikb[:, c, :], ALU.mult).then_inc(VE, 1)
            for h in range(H):
                lam_b = bass.AP(lam_col, h, [[16, 128], [0, TB]])
                e.scan(rkvb[:, h, :], lam_b, kvb[:, h, :], state2[:, h:h + 1])
                e.scan(rksb[:, h, :], lam_b, phikb[:, h, :],
                       state2[:, 16 + h:16 + h + 1])
            e.copy(state2[:, 0:16], rkvb[:, :, TB - 1:TB])
            e.copy(state2[:, 16:32], rksb[:, :, TB - 1:TB])
            e.nop_inc(VE, 1)
        raw.drain()
        raw.sem_inc(BAR, 1)

    pctx.close()

    S.base["gp"] += 1
    S.base["pe"] += 1 + 32 * NB
    S.base["ve"] += 2 + 33 * NB
    S.base["ac"] += 1 + 32 * NB


def phase_F2(nc, S, R, TT, bar_k=0):
    name = "phF2_"
    pctx = ExitStack()
    sb = lambda n, s, d=F32: pctx.enter_context(nc.sbuf_tensor(name + n, s, d))
    psal = lambda n, s, d=F32: pctx.enter_context(nc.psum_tensor(name + n, s, d))

    xn, rkvf, rksf, x1, acc1 = (TT["xn"], TT["rkvf"], TT["rksf"], TT["x1"],
                                TT["acc1"])
    den_hbm = TT["den_hbm"]

    wq_s = sb("wq", [128, 16, 2048], BF16)
    wo_s = sb("wo", [128, 16, 2048], BF16)
    xnb = sb("xnb", [128, 2, 16, 256], BF16)
    rkvs = sb("rkvs", [128, 16, 256], BF16)
    rkss = sb("rkss", [128, 16, 256], BF16)
    relb = sb("relb", [128, 2, 256])
    mnb = sb("mnb", [128, 2, 256])
    enb = sb("enb", [128, 2, 256])
    phiq = sb("phiq", [128, 256])
    zb = sb("zb", [128, 16, 256], BF16)
    y2b = sb("y2b", [128, 16, 256], BF16)
    ones_cb = sb("ones_cb", [128, 1], BF16)
    drow = sb("drow", [1, 256])
    rdrow = sb("rdrow", [1, 256])
    denc = sb("denc", [128, 2, 1])
    x1t = sb("x1t", [128, 2, 2048])

    pqA = psal("pqA", [128, 256]); pqB = psal("pqB", [128, 256])
    pdn = psal("pdn", [1, 256])
    pat = psal("pat", [128, 2048])

    ve0 = S.base["ve"]; ac0 = S.base["ac"]; pe0 = S.base["pe"]
    gp0 = S.base["gp"]; evz = S.base["ev"]
    VE, AC, PE, GP, EV = S["ve"], S["ac"], S["pe"], S["gp"], S["ev"]
    ldR, wtR, stR, x2R = R["ld"], R["wt"], R["st"], R["x2"]
    ld0, wt0, st0, x20 = ldR.n, wtR.n, stR.n, x2R.n
    BAR = S["bar"]
    barv = 5 * bar_k

    def pe_at(bi, k):
        return pe0 + 34 * bi + k

    def ve_at(bi, k):
        return ve0 + 35 * bi + k

    def ac_at(bi, k):
        return ac0 + 32 * bi + k

    block = pctx.enter_context(nc.Block(name))

    @block.sync
    def _(raw):
        raw.wait_ge(BAR, barv)
        for bi in range(NB):
            if bi >= 2:
                raw.wait_ge(PE, pe_at(bi - 2, 16))
            ldR.issue(raw, xnb[:, bi % 2, :, :], xn[bi].rearrange("c d t -> d c t"))
            if bi >= 1:
                raw.wait_ge(VE, ve_at(bi - 1, 32))
            ldR.issue(raw, rkvs[:, :, :], rkvf[bi].rearrange("c d t -> d c t"))
            ldR.issue(raw, rkss[:, :, :], rksf[bi].rearrange("c d t -> d c t"))
            for ti in range(2):
                j = 2 * bi + ti
                if j >= 2:
                    raw.wait_ge(VE, ve_at((j - 2) // 2, 34 + (j - 2) % 2))
                    stR.wait(raw, st0 + j - 2)
                ldR.issue(raw, x1t[:, j % 2, :], x1[128 * j:128 * (j + 1), :])
            x2R.wait(raw, x20 + bi)
            for ti in range(2):
                j = 2 * bi + ti
                ldR.issue(raw, denc[:, j % 2, :],
                          den_hbm[0, 128 * j:128 * (j + 1)].rearrange(
                              "(p o) -> p o", o=1),
                          allow_slow_non_contiguous=True)
        raw.drain()
        raw.sem_inc(BAR, 1)

    @block.gpsimd
    def _(raw):
        raw.wait_ge(BAR, barv)
        e = EW(raw)
        e.memset(ones_cb[:, :], 1.0).then_inc(GP, 1)
        for k in range(NCH):
            wtR.issue(raw, wq_s[:, k, :], TT["w_qkv"][128 * k:128 * (k + 1), 0:2048])
        for k in range(NCH):
            wtR.issue(raw, wo_s[:, k, :], TT["w_out"][128 * k:128 * (k + 1), :])
        raw.drain()
        raw.sem_inc(BAR, 1)

    @block.scalar
    def _(raw):
        raw.wait_ge(BAR, barv)
        e = EW(raw)
        for bi in range(NB):
            for c in range(NCH):
                p = c % 2
                e.wait(PE, pe_at(bi, c + 1))
                e.act(relb[:, p, :], (pqA if p == 0 else pqB)[:, :],
                      AF.Relu).then_inc(AC, 1)
                e.wait(VE, ve_at(bi, 2 * c + 1))
                e.act(enb[:, p, :], mnb[:, p, :], AF.Exp).then_inc(AC, 1)
            e.wait(VE, ve_at(bi, 33))
            x2R.issue(raw, den_hbm[0:1, 256 * bi:256 * (bi + 1)], rdrow[:, :])
            for ti in range(2):
                j = 2 * bi + ti
                e.wait(VE, ve_at(bi, 34 + ti))
                stR.issue(raw, acc1[128 * j:128 * (j + 1), :], x1t[:, j % 2, :])
        stR.wait_all(raw)
        raw.drain()
        raw.sem_inc(BAR, 1)

    @block.tensor
    def _(raw):
        raw.wait_ge(BAR, barv)
        for bi in range(NB):
            pb = bi % 2
            for c in range(NCH):
                if c == 0:
                    ldR.wait(raw, ld0 + 7 * bi)
                p = c % 2
                if 16 * bi + c >= 2:
                    raw.wait_ge(AC, ac_at(bi, 2 * (c - 2) + 1))
                    raw.wait_ge(VE, ve_at(bi, 2 * (c - 2) + 1))
                dst = (pqA if p == 0 else pqB)
                for k in range(NCH):
                    if bi == 0 and c == 0:
                        wtR.wait(raw, wt0 + k)
                    mmi = raw.matmul(dst[:, :], wq_s[:, k, 128 * c:128 * (c + 1)],
                                     xnb[:, pb, k, :], start=(k == 0),
                                     stop=(k == NCH - 1))
                mmi.then_inc(PE, 1)
            if bi >= 1:
                raw.wait_ge(VE, ve_at(bi - 1, 33))
            raw.wait_ge(GP, gp0 + 1)
            for c in range(NCH):
                raw.wait_ge(VE, ve_at(bi, 2 * c + 2))
                mmi = raw.matmul(pdn[0:1, :], ones_cb[:, 0:1], y2b[:, c, :],
                                 start=(c == 0), stop=(c == NCH - 1))
                mmi.then_inc(PE, 1)
            for ti in range(2):
                j = 2 * bi + ti
                if j >= 1:
                    raw.wait_ge(VE, ve_at((j - 1) // 2, 34 + (j - 1) % 2))
                for c in range(NCH):
                    raw.wait_ge(VE, ve_at(bi, 2 * c + 2))
                    if bi == 0 and c == 0:
                        wtR.wait_upto(raw, wt0 + 31)
                    for jj in range(4):
                        mmi = raw.matmul(pat[:, 512 * jj:512 * (jj + 1)],
                                         zb[:, c, 128 * ti:128 * (ti + 1)],
                                         wo_s[:, c, 512 * jj:512 * (jj + 1)],
                                         start=(c == 0), stop=(c == NCH - 1))
                mmi.then_inc(PE, 1)
        raw.drain()
        raw.sem_inc(BAR, 1)

    @block.vector
    def _(raw):
        raw.wait_ge(BAR, barv)
        e = EW(raw)
        for bi in range(NB):
            for c in range(NCH):
                p = c % 2
                e.wait(PE, pe_at(bi, c + 1))
                e.wait(AC, ac_at(bi, 2 * c + 1))
                e.tt(mnb[:, p, :], (pqA if p == 0 else pqB)[:, :],
                     relb[:, p, :], ALU.subtract).then_inc(VE, 1)
                e.wait(AC, ac_at(bi, 2 * c + 2))
                e.tt(phiq[:, :], enb[:, p, :], relb[:, p, :], ALU.add)
                if c == 0:
                    ldR.wait(raw, ld0 + 7 * bi + 1)
                    ldR.wait(raw, ld0 + 7 * bi + 2)
                if bi >= 1 and c == 0:
                    raw.wait_ge(PE, pe_at(bi - 1, 34))
                e.tt(zb[:, c, :], phiq[:, :], rkvs[:, c, :], ALU.mult)
                e.tt(y2b[:, c, :], phiq[:, :], rkss[:, c, :],
                     ALU.mult).then_inc(VE, 1)
            e.wait(PE, pe_at(bi, 32))
            e.ts(drow[:, :], pdn[0:1, :], 1e-6, None, ALU.max)
            e.recip(rdrow[:, :], drow[:, :])
            e.nop_inc(VE, 1)
            for ti in range(2):
                j = 2 * bi + ti
                e.wait(PE, pe_at(bi, 33 + ti))
                ldR.wait(raw, ld0 + 7 * bi + 3 + ti)
                ldR.wait(raw, ld0 + 7 * bi + 5 + ti)
                for jj in range(4):
                    e.stt(x1t[:, j % 2, 512 * jj:512 * (jj + 1)],
                          pat[:, 512 * jj:512 * (jj + 1)], denc[:, j % 2, 0:1],
                          x1t[:, j % 2, 512 * jj:512 * (jj + 1)],
                          ALU.mult, ALU.add)
                e.nop_inc(VE, 1)
        raw.drain()
        raw.sem_inc(BAR, 1)

    pctx.close()

    S.base["gp"] += 1
    S.base["pe"] += 34 * NB
    S.base["ve"] += 35 * NB
    S.base["ac"] += 32 * NB


def phase_G(nc, S, R, TT, bar_k=0):
    name = "phG_"
    pctx = ExitStack()
    sb = lambda n, s, d=F32: pctx.enter_context(nc.sbuf_tensor(name + n, s, d))
    psal = lambda n, s, d=F32: pctx.enter_context(nc.psum_tensor(name + n, s, d))

    xn, comb, acc1, out = TT["xn"], TT["comb"], TT["acc1"], TT["out"]
    w1c, w3c, w2c = TT["w1c"], TT["w3c"], TT["w2c"]
    NS = 4
    SPB = 4
    NU = 9

    w1u = sb("w1u", [128, 16, 1024], BF16)
    w3u = sb("w3u", [128, 16, 1024], BF16)
    w2u = sb("w2u", [128, 8, 2048], BF16)
    xns = sb("xns", [128, SPB, 16, 256], BF16)
    accs = sb("accs", [128, 8, 2048])
    gfm = sb("gfm", [128, 8, 256], BF16)
    hs = sb("hs", [128, 2, 256], BF16)
    combs = sb("combs", [128, 8, 8])

    phA_ = psal("phA", [128, 256]); phB_ = psal("phB", [128, 256])
    p3A = psal("p3A", [128, 256]); p3B = psal("p3B", [128, 256])
    eoA = psal("eoA", [128, 512]); eoB = psal("eoB", [128, 512])

    ve0 = S.base["ve"]; ac0 = S.base["ac"]; pe0 = S.base["pe"]
    VE, AC, PE = S["ve"], S["ac"], S["pe"]
    ldR, wtR, stR = R["ld"], R["wt"], R["st"]
    ld0, wt0, st0 = ldR.n, wtR.n, stR.n
    BAR = S["bar"]
    barv = 5 * bar_k

    def pe_at(s, u, k):
        return pe0 + 864 * s + 96 * u + k

    def ve_at(s, u, k):
        return ve0 + 576 * s + 64 * u + k

    def ac_at(s, u, k):
        return ac0 + 288 * s + 32 * u + k

    def ld_id(s, u=None, w=0):
        base = ld0 + 57 * s
        if u is None:
            return base
        return base + 12 + 5 * u + w

    CAST = 27

    def wt_acc_id(s, t):
        return wt0 + CAST + 8 * s + t

    def prev_unit(s, u):
        return (s, u - 1) if u >= 1 else (s - 1, 8)

    block = pctx.enter_context(nc.Block(name))

    @block.gpsimd
    def _(raw):
        raw.wait_ge(BAR, barv)
        for u in range(NU):
            wtR.issue(raw, w1c[2048 * u:2048 * (u + 1), :],
                      TT["w1a"][2048 * u:2048 * (u + 1), :])
            wtR.issue(raw, w3c[2048 * u:2048 * (u + 1), :],
                      TT["w3a"][2048 * u:2048 * (u + 1), :])
            wtR.issue(raw, w2c[1024 * u:1024 * (u + 1), :],
                      TT["w2a"][1024 * u:1024 * (u + 1), :])
        for s in range(NS):
            for t in range(8):
                j = 8 * s + t
                raw.wait_ge(VE, ve_at(s, 8, 16 * (t // 2) + 8 + 4 * (t % 2) + 4))
                wtR.issue(raw, accs[:, t, :], acc1[128 * j:128 * (j + 1), :],
                          accum_op=ALU.add)
        raw.drain()
        raw.sem_inc(BAR, 1)

    @block.sync
    def _(raw):
        raw.wait_ge(BAR, barv)
        for s in range(NS):
            if s >= 1:
                raw.wait_ge(PE, pe_at(s - 1, 8, 88))
            for b in range(SPB):
                ldR.issue(raw, xns[:, b, :, :],
                          xn[SPB * s + b].rearrange("c d t -> d c t"))
            for t in range(8):
                j = 8 * s + t
                if s >= 1:
                    raw.wait_ge(VE, ve_at(s - 1, 8, 16 * (t // 2) + 8 + 4 * (t % 2) + 4))
                ldR.issue(raw, combs[:, t, :], comb[128 * j:128 * (j + 1), :])
            for u in range(NU):
                if 9 * s + u >= 1:
                    sp, up = prev_unit(s, u)
                    raw.wait_ge(PE, pe_at(sp, up, 88))
                if s == 0:
                    wtR.wait(raw, wt0 + 3 * u)
                    wtR.wait(raw, wt0 + 3 * u + 1)
                w1src = w1c[2048 * u:2048 * (u + 1), :].rearrange(
                    "(k p) f -> p k f", p=128)
                w3src = w3c[2048 * u:2048 * (u + 1), :].rearrange(
                    "(k p) f -> p k f", p=128)
                ldR.issue(raw, w1u[:, :, 0:512], w1src[:, :, 0:512])
                ldR.issue(raw, w3u[:, :, 0:512], w3src[:, :, 0:512])
                ldR.issue(raw, w1u[:, :, 512:1024], w1src[:, :, 512:1024])
                ldR.issue(raw, w3u[:, :, 512:1024], w3src[:, :, 512:1024])
                if 9 * s + u >= 1:
                    sp, up = prev_unit(s, u)
                    raw.wait_ge(PE, pe_at(sp, up, 96))
                if s == 0:
                    wtR.wait(raw, wt0 + 3 * u + 2)
                ldR.issue(raw, w2u[:, :, :],
                          w2c[1024 * u:1024 * (u + 1), :].rearrange(
                              "(k p) f -> p k f", p=128))
        raw.drain()
        raw.sem_inc(BAR, 1)

    @block.tensor
    def _(raw):
        raw.wait_ge(BAR, barv)
        for s in range(NS):
            for u in range(NU):
                if u == 0:
                    for b in range(SPB):
                        ldR.wait(raw, ld_id(s) + b)
                ldR.wait(raw, ld_id(s, u, 0))
                ldR.wait(raw, ld_id(s, u, 1))
                for b in range(SPB):
                    for i in range(8):
                        ci = 8 * b + i
                        if b == 0 and i == 4:
                            ldR.wait(raw, ld_id(s, u, 2))
                            ldR.wait(raw, ld_id(s, u, 3))
                        dst = (phA_ if i % 2 == 0 else phB_)
                        if ci >= 2:
                            bb, ii = (ci - 2) // 8, (ci - 2) % 8
                            raw.wait_ge(AC, ac_at(s, u, 8 * bb + ii + 1))
                        elif 9 * s + u >= 1:
                            sp, up = prev_unit(s, u)
                            raw.wait_ge(AC, ac_at(sp, up, 31 + i))
                        for k in range(NCH):
                            mmi = raw.matmul(dst[:, :],
                                             w1u[:, k, 128 * i:128 * (i + 1)],
                                             xns[:, b, k, :], start=(k == 0),
                                             stop=(k == NCH - 1))
                        mmi.then_inc(PE, 1)
                        d3 = (p3A if i % 2 == 0 else p3B)
                        if ci >= 2:
                            bb, ii = (ci - 2) // 8, (ci - 2) % 8
                            raw.wait_ge(VE, ve_at(s, u, 16 * bb + ii + 1))
                        elif 9 * s + u >= 1:
                            sp, up = prev_unit(s, u)
                            raw.wait_ge(VE, ve_at(sp, up, 48 + 6 + i + 1))
                        for k in range(NCH):
                            mmi = raw.matmul(d3[:, :],
                                             w3u[:, k, 128 * i:128 * (i + 1)],
                                             xns[:, b, k, :], start=(k == 0),
                                             stop=(k == NCH - 1))
                        mmi.then_inc(PE, 1)
                    if b == 0:
                        ldR.wait(raw, ld_id(s, u, 4))
                    for ti in range(2):
                        for jj in range(4):
                            eidx = 4 * ti + jj
                            dst = (eoA if eidx % 2 == 0 else eoB)
                            gi = 8 * b + eidx
                            if gi >= 2:
                                bb, ee = (gi - 2) // 8, (gi - 2) % 8
                                raw.wait_ge(VE, ve_at(s, u, 16 * bb + 8 + ee + 1))
                            elif 9 * s + u >= 1:
                                sp, up = prev_unit(s, u)
                                raw.wait_ge(VE, ve_at(sp, up, 48 + 8 + 6 + eidx + 1))
                            for i in range(8):
                                if eidx == 0:
                                    raw.wait_ge(VE, ve_at(s, u, 16 * b + i + 1))
                                mmi = raw.matmul(
                                    dst[:, :], gfm[:, i, 128 * ti:128 * (ti + 1)],
                                    w2u[:, i, 512 * jj:512 * (jj + 1)],
                                    start=(i == 0), stop=(i == 7))
                            mmi.then_inc(PE, 1)
        raw.drain()
        raw.sem_inc(BAR, 1)

    @block.scalar
    def _(raw):
        raw.wait_ge(BAR, barv)
        e = EW(raw)
        for s in range(NS):
            for u in range(NU):
                for b in range(SPB):
                    for i in range(8):
                        ci = 8 * b + i
                        e.wait(PE, pe_at(s, u, 24 * b + 2 * i + 1))
                        if ci >= 2:
                            bb, ii = (ci - 2) // 8, (ci - 2) % 8
                            e.wait(VE, ve_at(s, u, 16 * bb + ii + 1))
                        elif 9 * s + u >= 1:
                            sp, up = prev_unit(s, u)
                            e.wait(VE, ve_at(sp, up, 48 + 6 + i + 1))
                        e.act(hs[:, i % 2, :],
                              (phA_ if i % 2 == 0 else phB_)[:, :],
                              AF.Silu).then_inc(AC, 1)
            for t in range(8):
                j = 8 * s + t
                wtR.wait(raw, wt_acc_id(s, t))
                stR.issue(raw, out[128 * j:128 * (j + 1), :], accs[:, t, :])
        stR.wait_all(raw)
        raw.drain()
        raw.sem_inc(BAR, 1)

    @block.vector
    def _(raw):
        raw.wait_ge(BAR, barv)
        e = EW(raw)
        for s in range(NS):
            for u in range(NU):
                for b in range(SPB):
                    for i in range(8):
                        e.wait(PE, pe_at(s, u, 24 * b + 2 * i + 2))
                        e.wait(AC, ac_at(s, u, 8 * b + i + 1))
                        if u == 0 and b == 0 and i == 0 and s >= 1:
                            for t in range(8):
                                stR.wait(raw, st0 + 8 * (s - 1) + t)
                        e.tt(gfm[:, i, :], (p3A if i % 2 == 0 else p3B)[:, :],
                             hs[:, i % 2, :], ALU.mult).then_inc(VE, 1)
                    for ti in range(2):
                        t = 2 * b + ti
                        for jj in range(4):
                            eidx = 4 * ti + jj
                            e.wait(PE, pe_at(s, u, 24 * b + 16 + eidx + 1))
                            dst = (eoA if eidx % 2 == 0 else eoB)
                            sl = accs[:, t, 512 * jj:512 * (jj + 1)]
                            if u == 0:
                                e.copy(sl, dst[:, :]).then_inc(VE, 1)
                            else:
                                if u == 1 and jj == 0:
                                    ldR.wait(raw, ld_id(s) + 4 + t)
                                e.stt(sl, dst[:, :], combs[:, t, u - 1:u],
                                      accs[:, t, 512 * jj:512 * (jj + 1)],
                                      ALU.mult, ALU.add).then_inc(VE, 1)
        raw.drain()
        raw.sem_inc(BAR, 1)

    pctx.close()

    S.base["pe"] += 864 * NS
    S.base["ve"] += 576 * NS
    S.base["ac"] += 288 * NS


def _prep_inputs(inputs):
    P = {}
    P["x"] = np.ascontiguousarray(inputs["L0_attn_x"], np.float32)
    P["wqT"] = np.ascontiguousarray(inputs["L0_attn_Wq"].T)
    P["wvT"] = np.ascontiguousarray(inputs["L0_attn_Wv"].T)
    P["wgT"] = np.ascontiguousarray(inputs["L0_attn_Wg"].T)
    P["woT"] = np.ascontiguousarray(inputs["L0_attn_Wo"].T)
    g = np.asarray(inputs["L0_attn_ln_gamma"], np.float32)
    b = np.asarray(inputs["L0_attn_ln_beta"], np.float32)
    P["gamma_l"] = np.ascontiguousarray(g.reshape(16, 128).T)
    P["beta_l"] = np.ascontiguousarray(b.reshape(16, 128).T)
    P["beta0"] = np.asarray(inputs["L0_attn_beta"], np.float32).reshape(1, 16)
    gw0 = np.asarray(inputs["L0_ffn_gate_W"], np.float32)
    P["gw0_l"] = np.ascontiguousarray(
        gw0.T.reshape(16, 128, 6).transpose(1, 0, 2).reshape(128, 96))
    P["W1"] = np.ascontiguousarray(inputs["L0_ffn_W1"], np.float32)
    P["b1"] = np.ascontiguousarray(inputs["L0_ffn_b1"], np.float32)
    P["W2"] = np.ascontiguousarray(inputs["L0_ffn_W2"], np.float32)
    P["b2"] = np.ascontiguousarray(inputs["L0_ffn_b2"], np.float32)
    P["w_qkv"] = np.ascontiguousarray(inputs["L1_attn_w_qkv"], np.float32)
    P["w_out"] = np.ascontiguousarray(inputs["L1_attn_w_out"], np.float32)
    P["decay1"] = np.asarray(inputs["L1_attn_decay_param"], np.float32).reshape(1, 16)
    gw1 = np.asarray(inputs["L1_ffn_gate_w"], np.float32)
    P["gw1_l"] = np.ascontiguousarray(
        gw1.T.reshape(16, 128, 8).transpose(1, 0, 2).reshape(128, 128))
    P["w1T"] = np.ascontiguousarray(inputs["L1_ffn_w1"].transpose(0, 2, 1))
    P["w3T"] = np.ascontiguousarray(inputs["L1_ffn_w3"].transpose(0, 2, 1))
    P["w2T"] = np.ascontiguousarray(inputs["L1_ffn_w2"].transpose(0, 2, 1))
    P["ws1T"] = np.ascontiguousarray(inputs["L1_ffn_ws1"].T)
    P["ws3T"] = np.ascontiguousarray(inputs["L1_ffn_ws3"].T)
    P["ws2T"] = np.ascontiguousarray(inputs["L1_ffn_ws2"].T)
    return P


FM_SHAPE = [NB, NCH, 128, TB]


def _build_one(phase_fn, specs, extra=None):
    nc = bass.Bass("TRN2", target_bir_lowering=False, debug=False)
    TT = {}
    outs = []
    for key, shape, dt, kind in specs:
        t = nc.dram_tensor(key, shape, dt, kind=kind)
        TT[key] = t.ap()
        if kind == "ExternalOutput":
            outs.append(key)
    with ExitStack() as ctx:
        S = Sems(nc, ctx)
        R = {"ld": DmaRing(nc, ctx, "ld", 2), "wt": DmaRing(nc, ctx, "wt", 4),
             "st": DmaRing(nc, ctx, "st", 2), "x2": DmaRing(nc, ctx, "x2", 2)}
        phase_fn(nc, S, R, TT)
    return nc, outs


_PHASES = None


def _phase_defs():
    EI, EO, IN = "ExternalInput", "ExternalOutput", "Internal"
    return [
        ("A", lambda nc, S, R, TT: phase_ln(nc, S, R, dict(TT, x=TT["x0"],
                                                           fm=TT["xn0f"],
                                                           gw=TT["gw0_l"]), "A"),
         [("x0", [T, D], F32, EI), ("gw0_l", [128, 96], F32, EI),
          ("xn0f", FM_SHAPE, BF16, EO), ("route_f", [1, 8], F32, EO),
          ("route_i", [1, 2], I32, EO)],
         {"x0": "x_i", "gw0_l": "gw0_l"}),
        ("B", lambda nc, S, R, TT: phase_B(nc, S, R, dict(TT, xn=TT["xn0f"])),
         [("xn0f", FM_SHAPE, BF16, EI), ("wqT", [D, D], F32, EI),
          ("wvT", [D, D], F32, EI), ("beta0", [1, 16], F32, EI),
          ("gamma_l", [128, 16], F32, EI), ("beta_l", [128, 16], F32, EI),
          ("a0f", FM_SHAPE, BF16, EO)],
         {"xn0f": "@xn0f", "wqT": "wqT", "wvT": "wvT", "beta0": "beta0",
          "gamma_l": "gamma_l", "beta_l": "beta_l"}),
        ("C", lambda nc, S, R, TT: phase_C(nc, S, R, dict(TT, xn=TT["xn0f"])),
         [("xn0f", FM_SHAPE, BF16, EI), ("a0f", FM_SHAPE, BF16, EI),
          ("x0", [T, D], F32, EI), ("wgT", [D, D], F32, EI),
          ("woT", [D, D], F32, EI), ("acc0", [T, D], F32, EO)],
         {"xn0f": "@xn0f", "a0f": "@a0f", "x0": "x_i", "wgT": "wgT",
          "woT": "woT"}),
        ("S1", lambda nc, S, R, TT: phase_S(nc, S, R,
                                            dict(TT, xn=TT["xn0f"],
                                                 acc_in=TT["acc0"],
                                                 acc_out=TT["accA"]), 0),
         [("xn0f", FM_SHAPE, BF16, EI), ("acc0", [T, D], F32, EI),
          ("W1", [6 * D, 1408], F32, EI), ("b1", [6, 1408], F32, EI),
          ("W2", [6 * 1408, D], F32, EI), ("b2", [6, D], F32, EI),
          ("route_i", [1, 2], I32, EI), ("route_f", [1, 8], F32, EI),
          ("accA", [T, D], F32, EO)],
         {"xn0f": "@xn0f", "acc0": "@acc0", "W1": "W1f", "b1": "b1",
          "W2": "W2f", "b2": "b2", "route_i": "@route_i",
          "route_f": "@route_f"}),
        ("S2", lambda nc, S, R, TT: phase_S(nc, S, R,
                                            dict(TT, xn=TT["xn0f"],
                                                 acc_in=TT["accA"],
                                                 acc_out=TT["x1"]), 1),
         [("xn0f", FM_SHAPE, BF16, EI), ("accA", [T, D], F32, EI),
          ("W1", [6 * D, 1408], F32, EI), ("b1", [6, 1408], F32, EI),
          ("W2", [6 * 1408, D], F32, EI), ("b2", [6, D], F32, EI),
          ("route_i", [1, 2], I32, EI), ("route_f", [1, 8], F32, EI),
          ("x1", [T, D], F32, EO)],
         {"xn0f": "@xn0f", "accA": "@accA", "W1": "W1f", "b1": "b1",
          "W2": "W2f", "b2": "b2", "route_i": "@route_i",
          "route_f": "@route_f"}),
        ("E", lambda nc, S, R, TT: phase_ln(nc, S, R, dict(TT, x=TT["x1"],
                                                           fm=TT["xn1f"],
                                                           gw=TT["gw1_l"]), "E"),
         [("x1", [T, D], F32, EI), ("gw1_l", [128, 128], F32, EI),
          ("xn1f", FM_SHAPE, BF16, EO), ("comb", [T, 8], F32, EO)],
         {"x1": "@x1", "gw1_l": "gw1_l"}),
        ("F1", lambda nc, S, R, TT: phase_F1(nc, S, R, dict(TT, xn=TT["xn1f"])),
         [("xn1f", FM_SHAPE, BF16, EI), ("w_qkv", [D, 6144], F32, EI),
          ("decay1", [1, 16], F32, EI),
          ("rkvf", FM_SHAPE, BF16, EO), ("rksf", FM_SHAPE, BF16, EO)],
         {"xn1f": "@xn1f", "w_qkv": "w_qkv", "decay1": "decay1"}),
        ("F2", lambda nc, S, R, TT: phase_F2(nc, S, R, dict(TT, xn=TT["xn1f"])),
         [("xn1f", FM_SHAPE, BF16, EI), ("rkvf", FM_SHAPE, BF16, EI),
          ("rksf", FM_SHAPE, BF16, EI), ("x1", [T, D], F32, EI),
          ("w_qkv", [D, 6144], F32, EI), ("w_out", [D, D], F32, EI),
          ("den_hbm", [1, T], F32, IN), ("acc1", [T, D], F32, EO)],
         {"xn1f": "@xn1f", "rkvf": "@rkvf", "rksf": "@rksf", "x1": "@x1",
          "w_qkv": "w_qkv", "w_out": "w_out"}),
        ("G", lambda nc, S, R, TT: phase_G(nc, S, R, dict(TT, xn=TT["xn1f"])),
         [("xn1f", FM_SHAPE, BF16, EI), ("comb", [T, 8], F32, EI),
          ("acc1", [T, D], F32, EI),
          ("w1a", [9 * 2048, 1024], F32, EI), ("w3a", [9 * 2048, 1024], F32, EI),
          ("w2a", [9 * 1024, 2048], F32, EI),
          ("w1c", [9 * 2048, 1024], BF16, IN), ("w3c", [9 * 2048, 1024], BF16, IN),
          ("w2c", [9 * 1024, 2048], BF16, IN),
          ("out", [T, D], F32, EO)],
         {"xn1f": "@xn1f", "comb": "@comb", "acc1": "@acc1", "w1a": "w1a",
          "w3a": "w3a", "w2a": "w2a"}),
    ]


def _build_fused():
    nc = bass.Bass("TRN2", target_bir_lowering=False, debug=False)
    EI, EO, IN = "ExternalInput", "ExternalOutput", "Internal"
    decls = [
        ("x0", [T, D], F32, EI), ("gw0_l", [128, 96], F32, EI),
        ("wqT", [D, D], F32, EI), ("wvT", [D, D], F32, EI),
        ("wgT", [D, D], F32, EI), ("woT", [D, D], F32, EI),
        ("beta0", [1, 16], F32, EI), ("gamma_l", [128, 16], F32, EI),
        ("beta_l", [128, 16], F32, EI),
        ("W1", [6 * D, 1408], F32, EI), ("b1", [6, 1408], F32, EI),
        ("W2", [6 * 1408, D], F32, EI), ("b2", [6, D], F32, EI),
        ("gw1_l", [128, 128], F32, EI),
        ("w_qkv", [D, 6144], F32, EI), ("w_out", [D, D], F32, EI),
        ("decay1", [1, 16], F32, EI),
        ("w1a", [9 * 2048, 1024], F32, EI), ("w3a", [9 * 2048, 1024], F32, EI),
        ("w2a", [9 * 1024, 2048], F32, EI),
        ("xn0f", FM_SHAPE, BF16, IN), ("route_f", [1, 8], F32, IN),
        ("route_i", [1, 2], I32, IN), ("a0f", FM_SHAPE, BF16, IN),
        ("acc0", [T, D], F32, IN), ("accA", [T, D], F32, IN),
        ("x1", [T, D], F32, IN), ("xn1f", FM_SHAPE, BF16, IN),
        ("comb", [T, 8], F32, IN), ("rkvf", FM_SHAPE, BF16, IN),
        ("rksf", FM_SHAPE, BF16, IN), ("den_hbm", [1, T], F32, IN),
        ("acc1", [T, D], F32, IN),
        ("w1c", [9 * 2048, 1024], BF16, IN), ("w3c", [9 * 2048, 1024], BF16, IN),
        ("w2c", [9 * 1024, 2048], BF16, IN),
        ("out", [T, D], F32, EO),
    ]
    TT = {}
    for key, sh, dt, kd in decls:
        TT[key] = nc.dram_tensor(key, sh, dt, kind=kd).ap()
    with ExitStack() as ctx:
        S = Sems(nc, ctx)
        R = {"ld": DmaRing(nc, ctx, "ld", 2), "wt": DmaRing(nc, ctx, "wt", 4),
             "st": DmaRing(nc, ctx, "st", 2), "x2": DmaRing(nc, ctx, "x2", 2)}
        phase_ln(nc, S, R, dict(TT, x=TT["x0"], fm=TT["xn0f"], gw=TT["gw0_l"]),
                 "A", bar_k=0)
        phase_B(nc, S, R, dict(TT, xn=TT["xn0f"]), bar_k=1)
        phase_C(nc, S, R, dict(TT, xn=TT["xn0f"]), bar_k=2)
        phase_S(nc, S, R, dict(TT, xn=TT["xn0f"], acc_in=TT["acc0"],
                               acc_out=TT["accA"]), 0, bar_k=3)
        phase_S(nc, S, R, dict(TT, xn=TT["xn0f"], acc_in=TT["accA"],
                               acc_out=TT["x1"]), 1, bar_k=4)
        phase_ln(nc, S, R, dict(TT, x=TT["x1"], fm=TT["xn1f"], gw=TT["gw1_l"]),
                 "E", bar_k=5)
        phase_F1(nc, S, R, dict(TT, xn=TT["xn1f"]), bar_k=6)
        phase_F2(nc, S, R, dict(TT, xn=TT["xn1f"]), bar_k=7)
        phase_G(nc, S, R, dict(TT, xn=TT["xn1f"]), bar_k=8)
    return nc


def kernel(**inputs):
    P = _prep_inputs(inputs)
    P["W1f"] = P["W1"].reshape(6 * D, 1408)
    P["W2f"] = P["W2"].reshape(6 * 1408, D)
    P["w1a"] = np.ascontiguousarray(
        np.concatenate([P["ws1T"], P["w1T"].reshape(8 * D, 1024)], axis=0))
    P["w3a"] = np.ascontiguousarray(
        np.concatenate([P["ws3T"], P["w3T"].reshape(8 * D, 1024)], axis=0))
    P["w2a"] = np.ascontiguousarray(
        np.concatenate([P["ws2T"], P["w2T"].reshape(8 * 1024, D)], axis=0))
    nc = _build_fused()
    names = ["gw0_l", "wqT", "wvT", "wgT", "woT", "beta0", "gamma_l", "beta_l",
             "b1", "b2", "gw1_l", "w_qkv", "w_out", "decay1", "w1a", "w3a", "w2a"]
    base = {n: P[n] for n in names}
    base["W1"] = P["W1f"]
    base["W2"] = P["W2f"]
    in_maps = [dict(base, x0=P["x"][i]) for i in range(NCORES)]
    res = run_bass_kernel_spmd(nc, in_maps, core_ids=list(range(NCORES)))
    return np.stack([np.asarray(res.results[i]["out"]) for i in range(NCORES)],
                    axis=0)


def kernel_unfused(**inputs):
    P = _prep_inputs(inputs)
    P["W1f"] = P["W1"].reshape(6 * D, 1408)
    P["W2f"] = P["W2"].reshape(6 * 1408, D)
    P["w1a"] = np.ascontiguousarray(
        np.concatenate([P["ws1T"], P["w1T"].reshape(8 * D, 1024)], axis=0))
    P["w3a"] = np.ascontiguousarray(
        np.concatenate([P["ws3T"], P["w3T"].reshape(8 * D, 1024)], axis=0))
    P["w2a"] = np.ascontiguousarray(
        np.concatenate([P["ws2T"], P["w2T"].reshape(8 * 1024, D)], axis=0))

    core_state = [dict() for _ in range(NCORES)]
    for pname, fn, specs, srcmap in _phase_defs():
        nc, outs = _build_one(fn, [(k, sh, dt, kd) for k, sh, dt, kd in specs])
        in_maps = []
        for i in range(NCORES):
            m = {}
            for key, sh, dt, kd in specs:
                if kd != "ExternalInput":
                    continue
                srcn = srcmap[key]
                if srcn == "x_i":
                    m[key] = P["x"][i]
                elif srcn.startswith("@"):
                    m[key] = core_state[i][srcn[1:]]
                else:
                    m[key] = P[srcn]
            in_maps.append(m)
        res = run_bass_kernel_spmd(nc, in_maps, core_ids=list(range(NCORES)))
        for i in range(NCORES):
            for o in outs:
                core_state[i]["" + o] = np.asarray(res.results[i][o])
    return np.stack([core_state[i]["out"] for i in range(NCORES)], axis=0)


# version 09
# speedup vs baseline: 1.0732x; 1.0732x over previous
import numpy as np
import ml_dtypes
from contextlib import ExitStack

import concourse.bass as bass
import concourse.mybir as mybir
from concourse.bass_utils import run_bass_kernel_spmd
from concourse.expressions import RuntimeValue

F32 = mybir.dt.float32
BF16 = mybir.dt.bfloat16
I32 = mybir.dt.int32
AF = mybir.ActivationFunctionType
ALU = mybir.AluOpType
AX = mybir.AxisListType
ds = bass.ds
BF16NP = ml_dtypes.bfloat16

NCORES = 8
T = 4096
D = 2048
H = 16
HD = 128
NT = T // 128
TB = 256
NB = T // TB
NCH = D // 128
LN2 = 0.6931471805599453
TINY = 1.1754943508222875e-38


class EW:

    def __init__(self, eng):
        self.e = eng
        self._dw = []
        self._dr = []

    @staticmethod
    def _foot(ap):
        t = getattr(ap, "tensor", None)
        name = t.name if t is not None else getattr(ap, "name", repr(ap))
        try:
            off = int(ap.offset)
            apl = [(int(s), int(c)) for s, c in ap.ap]
            span = 1
            for s, c in apl[1:]:
                span += abs(s) * (c - 1)
            if len(apl) == 1:
                span = 1
            return (name, off, off + span)
        except Exception:
            return (name, 0, 1 << 60)

    @classmethod
    def _feet(cls, aps):
        return [cls._foot(ap) for ap in aps
                if ap is not None and not isinstance(ap, (int, float))]

    @staticmethod
    def _overlap(a, b):
        for (n1, l1, h1) in a:
            for (n2, l2, h2) in b:
                if n1 == n2 and l1 < h2 and l2 < h1:
                    return True
        return False

    def _emit(self, reads, writes, custom, fn):
        rf = self._feet(reads)
        wf = self._feet(writes)
        if (self._overlap(rf, self._dw) or self._overlap(wf, self._dw)
                or self._overlap(wf, self._dr)):
            self.e.drain()
            self._dw = []
            self._dr = []
        r = fn()
        self._dw += wf
        self._dr += rf
        return r

    def copy(self, out, in_):
        return self._emit([in_], [out], False, lambda: self.e.tensor_copy(out, in_))

    def tt(self, out, a, b, op):
        return self._emit([a, b], [out], False,
                          lambda: self.e.tensor_tensor(out, a, b, op=op))

    def memset(self, ap, v):
        return self._emit([], [ap], False, lambda: self.e.memset(ap, v))

    def iota(self, ap, pattern, base=0, cm=0):
        return self._emit([], [ap], True,
                          lambda: self.e.iota(ap, pattern, base=base,
                                              channel_multiplier=cm,
                                              allow_small_or_imprecise_dtypes=True))

    def ts(self, out, in0, s1, s2, op0, op1=ALU.add, accum=None):
        custom = not isinstance(s1, (int, float)) or accum is not None
        if s2 is None:
            s2 = 0.0
        reads = [in0] + ([s1] if not isinstance(s1, (int, float)) else [])
        writes = [out] + ([accum] if accum is not None else [])
        return self._emit(reads, writes, custom,
                          lambda: self.e.tensor_scalar(out=out, in0=in0, scalar1=s1,
                                                       scalar2=s2, op0=op0, op1=op1,
                                                       accum_out=accum))

    def stt(self, out, in0, s, in1, op0, op1, accum=None):
        reads = [in0, in1] + ([s] if not isinstance(s, (int, float)) else [])
        writes = [out] + ([accum] if accum is not None else [])
        return self._emit(reads, writes, True,
                          lambda: self.e.scalar_tensor_tensor(out=out, in0=in0, scalar=s,
                                                              in1=in1, op0=op0, op1=op1,
                                                              accum_out=accum))

    def reduce(self, out, in_, op):
        return self._emit([in_], [out], True,
                          lambda: self.e.tensor_reduce(out, in_, axis=AX.X, op=op))

    def recip(self, out, in_):
        return self._emit([in_], [out], True, lambda: self.e.reciprocal(out, in_))

    def scan(self, out, d0, d1, init, op0=ALU.mult, op1=ALU.add):
        reads = [d0, d1] + ([init] if not isinstance(init, (int, float)) else [])
        return self._emit(reads, [out], True,
                          lambda: self.e.tensor_tensor_scan(out=out, data0=d0, data1=d1,
                                                            initial=init, op0=op0,
                                                            op1=op1))

    def act(self, out, in_, func, bias=0.0, scale=1.0, accum=None, check=True):
        custom = accum is not None or not isinstance(bias, (int, float))
        reads = [in_] + ([bias] if not isinstance(bias, (int, float)) else [])
        writes = [out] + ([accum] if accum is not None else [])
        return self._emit(reads, writes, custom,
                          lambda: self.e.activation(out=out, in_=in_, func=func,
                                                    bias=bias, scale=scale,
                                                    accum_out=accum))

    def wait(self, sem, v):
        return self.e.wait_ge(sem, v)

    def inc(self, sem, v=1):
        return self.e.sem_inc(sem, v)

    def nop_inc(self, sem, v=1):
        self.e.drain()
        self._dw = []
        self._dr = []
        return self.e.nop().then_inc(sem, v)

    def drain(self):
        self.e.drain()
        self._dw = []
        self._dr = []


SEM_NAMES = ["pe", "ve", "ac", "gp", "ev", "bar"]


class Sems:
    def __init__(self, nc, ctx):
        self.h = {n: ctx.enter_context(nc.semaphore(n)) for n in SEM_NAMES}
        self.base = {n: 0 for n in SEM_NAMES}

    def __getitem__(self, n):
        return self.h[n]


class DmaRing:

    def __init__(self, nc, ctx, name, depth):
        self.sems = [ctx.enter_context(nc.semaphore(f"{name}{j}"))
                     for j in range(depth)]
        self.depth = depth
        self.n = 0

    def issue(self, raw, out, in_, **kw):
        j = self.n % self.depth
        k = self.n // self.depth
        if k > 0:
            raw.wait_ge(self.sems[j], 16 * k)
        raw.dma_start(out=out, in_=in_, **kw).then_inc(self.sems[j], 16)
        self.n += 1
        return self.n - 1

    def wait(self, raw, dma_id):
        if dma_id < 0:
            return
        j = dma_id % self.depth
        raw.wait_ge(self.sems[j], 16 * (dma_id // self.depth + 1))

    def wait_upto(self, raw, last_id):
        for j in range(min(self.depth, last_id + 1)):
            idj = last_id - ((last_id - j) % self.depth)
            raw.wait_ge(self.sems[j], 16 * (idj // self.depth + 1))

    def wait_all(self, raw):
        for j in range(self.depth):
            if self.n > j:
                cnt = (self.n - 1 - j) // self.depth + 1
                raw.wait_ge(self.sems[j], 16 * cnt)


def phase_ln(nc, S, R, TT, mode, bar_k=0):
    is_A = (mode == "A")
    name = "phA_" if is_A else "phE_"
    pctx = ExitStack()
    sb = lambda n, s, d=F32: pctx.enter_context(nc.sbuf_tensor(name + n, s, d))
    psal = lambda n, s, d=F32: pctx.enter_context(nc.psum_tensor(name + n, s, d))

    x0, fm = TT["x"], TT["fm"]

    xt = sb("xt", [128, 2, 2048])
    stat = sb("stat", [128, 8])
    sqs = sb("sqs", [128, 2048])
    fmt = sb("fmt", [128, 2, 16, 256], BF16)
    ident = sb("ident", [128, 128])
    io_a = sb("io_a", [128, 128])
    io_b = sb("io_b", [128, 128])
    ptrA = psal("ptrA", [128, 512])
    ptrB = psal("ptrB", [128, 512])
    epsb = sb("epsb", [128, 1])

    if is_A:
        racc = sb("racc", [128, 16])
        racct = sb("racct", [128, 2, 16])
        gwl = sb("gwl", [128, 96])
        r6 = sb("r6", [1, 48])
        rr = sb("rr", [1, 8])
        routef_s = sb("routef", [1, 8])
        routei_s = sb("routei", [1, 2], I32)
        iota6 = sb("iota6", [1, 6])
        pgate = psal("pgate", [1, 6])
    else:
        gwl = sb("gwl", [128, 128], BF16)
        combt = sb("combt", [128, 2, 8])
        combsc = sb("combsc", [128, 2, 8])
        mrow = sb("mrow", [128, 4])
        pscore = psal("pscore", [128, 2, 8])

    ve0 = S.base["ve"]; ac0 = S.base["ac"]
    pe0 = S.base["pe"]; gp0 = S.base["gp"]; evz = S.base["ev"]
    VE, AC, PE, GP, EV = S["ve"], S["ac"], S["pe"], S["gp"], S["ev"]
    ldR, wtR, stR, x2R = R["ld"], R["wt"], R["st"], R["x2"]
    ld0, wt0, st0, x20 = ldR.n, wtR.n, stR.n, x2R.n
    VPT = 3 if is_A else 5
    APT = 1 if is_A else 2
    PPT = 16 if is_A else 17
    BAR = S["bar"]
    barv = 5 * bar_k

    block = pctx.enter_context(nc.Block(name))

    @block.sync
    def _(raw):
        raw.wait_ge(BAR, barv)
        if is_A:
            ldR.issue(raw, gwl[:, :], TT["gw"])
        for i in range(NT):
            if i >= 2:
                raw.wait_ge(PE, pe0 + PPT * (i - 2) + 16)
                raw.wait_ge(AC, ac0 + APT * (i - 2) + 1)
            ldR.issue(raw, xt[:, i % 2, :], x0[128 * i:128 * (i + 1), :])
        raw.drain()
        raw.sem_inc(BAR, 1)

    @block.gpsimd
    def _(raw):
        raw.wait_ge(BAR, barv)
        e = EW(raw)
        e.iota(io_a[:, :], [[1, 128]], base=0, cm=0)
        e.iota(io_b[:, :], [[0, 128]], base=0, cm=1)
        if is_A:
            e.iota(iota6[:, :], [[1, 6]], base=0, cm=0)
        else:
            wtR.issue(raw, gwl[:, :], TT["gw"])
        e.memset(epsb[:, :], 1e-5).then_inc(GP, 1)
        raw.drain()
        raw.sem_inc(BAR, 1)

    @block.vector
    def _(raw):
        raw.wait_ge(BAR, barv)
        e = EW(raw)
        e.wait(GP, gp0 + 1)
        e.tt(ident[:, :], io_a[:, :], io_b[:, :], ALU.is_equal)
        if is_A:
            e.memset(racc[:, :], 0.0)
        ldsk = 1 if is_A else 0
        for i in range(NT):
            p = i % 2
            tile = xt[:, p, :]
            mu = stat[:, 4 * p + 0:4 * p + 1]
            rstd = stat[:, 4 * p + 2:4 * p + 3]
            sd = stat[:, 4 * p + 3:4 * p + 4]
            if is_A and i >= 2:
                e.wait(EV, evz + 16 * (i - 1))
                e.tt(racc[:, :], racc[:, :], racct[:, i % 2, :], ALU.add)
            e.nop_inc(VE, 1)
            ldR.wait(raw, ld0 + ldsk + i)
            e.reduce(mu, tile, ALU.add)
            e.ts(mu, mu, 1.0 / D, 0.0, ALU.mult)
            e.ts(tile, tile, mu, 0.0, ALU.subtract)
            e.nop_inc(VE, 1)
            e.wait(AC, ac0 + APT * i + 1)
            e.recip(rstd, sd)
            e.ts(tile, tile, rstd, 0.0, ALU.mult)
            e.nop_inc(VE, 1)
            if not is_A:
                e.wait(PE, pe0 + PPT * i + 17)
                sc_ps = pscore[:, i % 2, :]
                m1 = mrow[:, 0:1]; m2 = mrow[:, 1:2]; ssum = mrow[:, 2:3]
                lg = combsc[:, i % 2, :]
                e.reduce(m1, sc_ps, ALU.max)
                e.ts(lg, sc_ps, m1, 0.0, ALU.subtract)
                e.nop_inc(VE, 1)
                e.wait(AC, ac0 + 2 * i + 2)
                e.reduce(ssum, lg, ALU.add)
                e.recip(ssum, ssum)
                prb = combt[:, i % 2, :]
                if i >= 2:
                    x2R.wait(raw, x20 + i - 2)
                e.ts(prb, lg, ssum, 0.0, ALU.mult)
                e.reduce(m1, prb, ALU.max)
                e.ts(lg, prb, m1, 0.0, ALU.is_equal)
                e.stt(lg, lg, -2.0, prb, ALU.mult, ALU.add)
                e.reduce(m2, lg, ALU.max)
                e.ts(lg, prb, m2, 0.0, ALU.is_ge)
                e.tt(prb, prb, lg, ALU.mult)
                e.nop_inc(VE, 1)
        if is_A:
            e.wait(EV, evz + 16 * (NT - 1))
            e.tt(racc[:, :], racc[:, :], racct[:, (NT - 2) % 2, :], ALU.add)
            e.wait(EV, evz + 16 * NT)
            e.tt(racc[:, :], racc[:, :], racct[:, (NT - 1) % 2, :], ALU.add)
            e.nop_inc(VE, 1)
            e.wait(PE, pe0 + 16 * NT + 1)
            lgt = r6[0:1, 0:6]
            e.ts(lgt, pgate[0:1, :], 1.0 / T, 0.0, ALU.mult)
            m1 = rr[0:1, 0:1]
            e.reduce(m1, lgt, ALU.max)
            e.ts(rr[0:1, 1:2], m1, -1.0, 0.0, ALU.mult)
            e.nop_inc(VE, 1)
            e.wait(AC, ac0 + NT + 1)
            e6 = r6[0:1, 8:14]
            s1 = rr[0:1, 2:3]
            e.reduce(s1, e6, ALU.add)
            e.recip(s1, s1)
            probs = r6[0:1, 16:22]
            e.ts(probs, e6, s1, 0.0, ALU.mult)
            p1 = rr[0:1, 3:4]
            e.reduce(p1, probs, ALU.max)
            eq = r6[0:1, 24:30]
            e.ts(eq, probs, p1, 0.0, ALU.is_equal)
            t6 = r6[0:1, 32:38]
            e.tt(t6, eq, iota6[0:1, :], ALU.mult)
            e.reduce(routef_s[0:1, 0:1], t6, ALU.add)
            msk = r6[0:1, 40:46]
            e.stt(msk, eq, -2.0, probs, ALU.mult, ALU.add)
            p2 = rr[0:1, 4:5]
            e.reduce(p2, msk, ALU.max)
            eq2 = r6[0:1, 24:30]
            e.ts(eq2, probs, p2, 0.0, ALU.is_equal)
            e.tt(t6, eq2, iota6[0:1, :], ALU.mult)
            e.reduce(routef_s[0:1, 1:2], t6, ALU.add)
            ws = rr[0:1, 5:6]
            e.tt(ws, p1, p2, ALU.add)
            e.recip(ws, ws)
            e.ts(routef_s[0:1, 2:3], p1, ws, 0.0, ALU.mult)
            e.ts(routef_s[0:1, 3:4], p2, ws, 0.0, ALU.mult)
            e.copy(routef_s[0:1, 4:5], p1)
            e.copy(routef_s[0:1, 5:6], p2)
            e.memset(routef_s[0:1, 6:8], 0.0)
            e.drain()
            e.copy(routei_s[0:1, 0:2], routef_s[0:1, 0:2])
            e.nop_inc(VE, 1)
        raw.drain()
        raw.sem_inc(BAR, 1)

    @block.tensor
    def _(raw):
        raw.wait_ge(BAR, barv)
        for i in range(NT):
            raw.wait_ge(VE, ve0 + VPT * i + 3)
            p = i % 2
            for c in range(NCH):
                gidx = 16 * i + c
                if gidx >= 2:
                    raw.wait_ge(EV, evz + gidx - 1)
                dst = (ptrA if c % 2 == 0 else ptrB)
                raw.transpose(dst[:, 0:128], xt[:, p, 128 * c:128 * (c + 1)],
                              ident[:, :]).then_inc(PE, 1)
            if not is_A:
                if i == 0:
                    wtR.wait(raw, wt0)
                if i >= 2:
                    raw.wait_ge(VE, ve0 + 5 * (i - 2) + 5)
                bi, ti = i // 2, i % 2
                pb = bi % 2
                for c in range(NCH):
                    raw.wait_ge(EV, evz + 16 * i + c + 1)
                    mmi = raw.matmul(pscore[:, i % 2, :],
                                     fmt[:, pb, c, 128 * ti:128 * ti + 128],
                                     gwl[:, 8 * c:8 * (c + 1)],
                                     start=(c == 0), stop=(c == NCH - 1))
                mmi.then_inc(PE, 1)
        if is_A:
            raw.wait_ge(VE, ve0 + 3 * NT + 1)
            for c in range(NCH):
                mmi = raw.matmul(pgate[0:1, :], racc[:, c:c + 1],
                                 gwl[:, 6 * c:6 * (c + 1)],
                                 start=(c == 0), stop=(c == NCH - 1))
            mmi.then_inc(PE, 1)
        raw.drain()
        raw.sem_inc(BAR, 1)

    @block.scalar
    def _(raw):
        raw.wait_ge(BAR, barv)
        e = EW(raw)
        for i in range(NT):
            p = i % 2
            tile = xt[:, p, :]
            var = stat[:, 4 * p + 1:4 * p + 2]
            sd = stat[:, 4 * p + 3:4 * p + 4]
            e.wait(VE, ve0 + VPT * i + 2)
            e.act(sqs[:, :], tile, AF.Square, accum=var)
            if i == 0:
                e.wait(GP, gp0 + 1)
            e.act(sd, var, AF.Sqrt, bias=epsb[:, 0:1], scale=1.0 / D).then_inc(AC, 1)
            bi, ti = i // 2, i % 2
            pb = bi % 2
            if ti == 0 and bi >= 2:
                stR.wait(raw, st0 + bi - 2)
            if is_A:
                e.wait(VE, ve0 + 3 * i + 1)
            for c in range(NCH):
                e.wait(PE, pe0 + PPT * i + c + 1)
                src = (ptrA if c % 2 == 0 else ptrB)
                acc = racct[:, p, c:c + 1] if is_A else None
                e.act(fmt[:, pb, c, 128 * ti:128 * ti + 128], src[:, 0:128],
                      AF.Copy, accum=acc, check=False).then_inc(EV, 1)
            if ti == 1:
                raw.wait_ge(EV, evz + 16 * (i + 1))
                stR.issue(raw, fm[bi].rearrange("c d t -> d c t"), fmt[:, pb, :, :])
            if not is_A:
                e.wait(VE, ve0 + 5 * i + 4)
                e.act(combsc[:, i % 2, :], combsc[:, i % 2, :],
                      AF.Exp).then_inc(AC, 1)
                e.wait(VE, ve0 + 5 * i + 5)
                x2R.issue(raw, TT["comb"][128 * i:128 * (i + 1), :],
                          combt[:, i % 2, :])
        if is_A:
            e.wait(VE, ve0 + 3 * NT + 2)
            e.act(r6[0:1, 8:14], r6[0:1, 0:6], AF.Exp,
                  bias=rr[0:1, 1:2]).then_inc(AC, 1)
            e.wait(VE, ve0 + 3 * NT + 3)
            x2R.issue(raw, TT["route_f"], routef_s[0:1, :])
            x2R.issue(raw, TT["route_i"], routei_s[0:1, :])
        x2R.wait_all(raw)
        stR.wait_all(raw)
        raw.drain()
        raw.sem_inc(BAR, 1)

    pctx.close()

    S.base["gp"] += 1
    S.base["pe"] += (16 * NT + 1) if is_A else 17 * NT
    S.base["ev"] += 16 * NT
    S.base["ve"] += (3 * NT + 3) if is_A else 5 * NT
    S.base["ac"] += (NT + 1) if is_A else 2 * NT


def phase_B(nc, S, R, TT, bar_k=0):
    name = "phB_"
    pctx = ExitStack()
    sb = lambda n, s, d=F32: pctx.enter_context(nc.sbuf_tensor(name + n, s, d))
    psal = lambda n, s, d=F32: pctx.enter_context(nc.psum_tensor(name + n, s, d))

    xn, a0f = TT["xn"], TT["a0f"]

    wq_s = sb("wq", [128, 16, 2048], BF16)
    wv_s = sb("wv", [128, 16, 2048], BF16)
    xnb = sb("xnb", [128, 2, 16, 256], BF16)
    vfm = sb("vfm", [128, 16, 256], BF16)
    retb = sb("retb", [128, 16, 256])
    sqb = sb("sqb", [128, 2, 256])
    t1b = sb("t1b", [128, 256])
    a0b = sb("a0b", [128, 16, 256], BF16)
    lam_col = sb("lam", [128, 16])
    state = sb("state", [128, 16])
    gam = sb("gam", [128, 16])
    bet = sb("bet", [128, 16])
    ones_c = sb("ones_c", [128, 1])
    ones_r = sb("ones_r", [1, 128])
    brow = sb("brow", [1, 16])
    lrow = sb("lrow", [1, 16])
    murow = sb("murow", [1, 256])
    varrow = sb("varrow", [1, 256])
    sdrow = sb("sdrow", [1, 256])
    rstdrow = sb("rstdrow", [1, 256])
    epsb = sb("epsb", [1, 1])

    pvA = psal("pvA", [128, 256]); pvB = psal("pvB", [128, 256])
    pqA = psal("pqA", [128, 256]); pqB = psal("pqB", [128, 256])
    psu = psal("psu", [1, 256]); psq = psal("psq", [1, 256])
    pmu = psal("pmu", [128, 256]); prs = psal("prs", [128, 256])

    ve0 = S.base["ve"]; ac0 = S.base["ac"]; pe0 = S.base["pe"]
    gp0 = S.base["gp"]; evz = S.base["ev"]
    VE, AC, PE, GP, EV = S["ve"], S["ac"], S["pe"], S["gp"], S["ev"]
    ldR, wtR, stR = R["ld"], R["wt"], R["st"]
    ld0, wt0, st0 = ldR.n, wtR.n, stR.n
    BAR = S["bar"]
    barv = 5 * bar_k
    PEs, VEs, ACs = pe0 + 1, ve0 + 2, ac0 + 1

    def pe_at(bi, k):
        return PEs + 50 * bi + k

    def ve_at(bi, k):
        return VEs + 19 * bi + k

    def ac_at(bi, k):
        return ACs + 17 * bi + k

    block = pctx.enter_context(nc.Block(name))

    @block.sync
    def _(raw):
        raw.wait_ge(BAR, barv)
        ldR.issue(raw, brow[:, :], TT["beta0"])
        ldR.issue(raw, gam[:, :], TT["gamma_l"])
        ldR.issue(raw, bet[:, :], TT["beta_l"])
        for bi in range(NB):
            if bi >= 2:
                raw.wait_ge(PE, pe_at(bi - 2, 32))
            ldR.issue(raw, xnb[:, bi % 2, :, :],
                      xn[bi].rearrange("c d t -> d c t"))
        raw.drain()
        raw.sem_inc(BAR, 1)

    @block.gpsimd
    def _(raw):
        raw.wait_ge(BAR, barv)
        e = EW(raw)
        e.memset(ones_c[:, :], 1.0)
        e.memset(ones_r[:, :], 1.0)
        e.memset(state[:, :], 0.0)
        e.memset(epsb[:, :], 1e-5).then_inc(GP, 1)
        for k in range(NCH):
            wtR.issue(raw, wv_s[:, k, :], TT["wvT"][128 * k:128 * (k + 1), :])
        for k in range(NCH):
            wtR.issue(raw, wq_s[:, k, :], TT["wqT"][128 * k:128 * (k + 1), :])
        raw.drain()
        raw.sem_inc(BAR, 1)

    @block.scalar
    def _(raw):
        raw.wait_ge(BAR, barv)
        e = EW(raw)
        ldR.wait(raw, ld0)
        e.act(lrow[:, :], brow[:, :], AF.Exp, scale=-LN2).then_inc(AC, 1)
        for bi in range(NB):
            pb = bi % 2
            if bi >= 1:
                e.wait(VE, ve_at(bi - 1, 1))
            for c in range(NCH):
                e.wait(PE, pe_at(bi, c + 1))
                e.act(vfm[:, c, :], (pvA if c % 2 == 0 else pvB)[:, :],
                      AF.Copy).then_inc(EV, 1)
            for c in range(NCH):
                e.wait(VE, ve_at(bi, c + 1))
                if c >= 2:
                    e.wait(PE, pe_at(bi, 33 + c - 1))
                e.act(sqb[:, c % 2, :], retb[:, c, :], AF.Square).then_inc(AC, 1)
            e.wait(VE, ve_at(bi, 17))
            e.act(sdrow[:, :], varrow[:, :], AF.Sqrt,
                  bias=epsb[0:1, 0:1]).then_inc(AC, 1)
            e.wait(VE, ve_at(bi, 19))
            stR.issue(raw, a0f[bi].rearrange("c d t -> d c t"), a0b[:, :, :])
        stR.wait_all(raw)
        raw.drain()
        raw.sem_inc(BAR, 1)

    @block.tensor
    def _(raw):
        raw.wait_ge(BAR, barv)
        raw.wait_ge(VE, VEs - 1)
        raw.wait_ge(GP, gp0 + 1)
        raw.matmul(pmu[:, 0:16], ones_r[0:1, :], lrow[0:1, :],
                   start=True, stop=True).then_inc(PE, 1)
        for bi in range(NB):
            pb = bi % 2
            for c in range(NCH):
                if 16 * bi + c >= 2:
                    raw.wait_ge(EV, evz + 16 * bi + c - 1)
                dst = (pvA if c % 2 == 0 else pvB)
                if c == 0:
                    ldR.wait(raw, ld0 + 3 + bi)
                for k in range(NCH):
                    if bi == 0 and c == 0:
                        wtR.wait(raw, wt0 + k)
                    mmi = raw.matmul(dst[:, :], wv_s[:, k, 128 * c:128 * (c + 1)],
                                     xnb[:, pb, k, :], start=(k == 0),
                                     stop=(k == NCH - 1))
                mmi.then_inc(PE, 1)
            for c in range(NCH):
                if c >= 2:
                    raw.wait_ge(VE, ve_at(bi, c - 1))
                elif bi >= 1:
                    raw.wait_ge(VE, ve_at(bi - 1, 14 + c + 1))
                dst = (pqA if c % 2 == 0 else pqB)
                for k in range(NCH):
                    if bi == 0 and c == 0:
                        wtR.wait(raw, wt0 + 16 + k)
                    mmi = raw.matmul(dst[:, :], wq_s[:, k, 128 * c:128 * (c + 1)],
                                     xnb[:, pb, k, :], start=(k == 0),
                                     stop=(k == NCH - 1))
                mmi.then_inc(PE, 1)
            if bi >= 1:
                raw.wait_ge(VE, ve_at(bi - 1, 17))
            for c in range(NCH):
                raw.wait_ge(VE, ve_at(bi, c + 1))
                mmi = raw.matmul(psu[0:1, :], ones_c[:, 0:1], retb[:, c, :],
                                 start=(c == 0), stop=(c == NCH - 1))
            mmi.then_inc(PE, 1)
            for c in range(NCH):
                raw.wait_ge(AC, ac_at(bi, c + 1))
                mmi = raw.matmul(psq[0:1, :], ones_c[:, 0:1], sqb[:, c % 2, :],
                                 start=(c == 0), stop=(c == NCH - 1))
                mmi.then_inc(PE, 1)
            raw.wait_ge(VE, ve_at(bi, 18))
            if bi >= 1:
                raw.wait_ge(VE, ve_at(bi - 1, 19))
            else:
                raw.wait_ge(VE, VEs)
            raw.matmul(pmu[:, :], ones_r[0:1, :], murow[0:1, :],
                       start=True, stop=True)
            mmi = raw.matmul(prs[:, :], ones_r[0:1, :], rstdrow[0:1, :],
                             start=True, stop=True)
            mmi.then_inc(PE, 1)
        raw.drain()
        raw.sem_inc(BAR, 1)

    @block.vector
    def _(raw):
        raw.wait_ge(BAR, barv)
        e = EW(raw)
        e.wait(AC, ACs)
        e.ts(lrow[:, :], lrow[:, :], -1.0, 1.0, ALU.mult, ALU.add)
        e.ts(lrow[:, :], lrow[:, :], TINY, None, ALU.max)
        e.ts(lrow[:, :], lrow[:, :], 1.0 - 1e-9, None, ALU.min)
        e.nop_inc(VE, 1)
        e.wait(PE, PEs)
        e.copy(lam_col[:, :], pmu[:, 0:16])
        ldR.wait(raw, ld0 + 2)
        e.nop_inc(VE, 1)
        for bi in range(NB):
            for h in range(H):
                e.wait(EV, evz + 16 * bi + h + 1)
                if bi >= 1 and h == 0:
                    e.wait(AC, ac_at(bi - 1, 16))
                    e.wait(PE, pe_at(bi - 1, 33))
                lam_b = bass.AP(lam_col.tensor if hasattr(lam_col, "tensor")
                                else lam_col, h, [[16, 128], [0, TB]])
                e.scan(retb[:, h, :], lam_b, vfm[:, h, :], state[:, h:h + 1])
            e.copy(state[:, :], retb[:, :, TB - 1:TB])
            for c in range(NCH):
                e.wait(PE, pe_at(bi, 16 + c + 1))
                src = (pqA if c % 2 == 0 else pqB)
                e.tt(retb[:, c, :], src[:, :], retb[:, c, :],
                     ALU.mult).then_inc(VE, 1)
            e.wait(PE, pe_at(bi, 33))
            e.ts(murow[:, :], psu[0:1, :], 1.0 / D, 0.0, ALU.mult)
            e.wait(PE, pe_at(bi, 49))
            e.ts(varrow[:, :], psq[0:1, :], 1.0 / D, 0.0, ALU.mult)
            e.tt(sdrow[:, :], murow[:, :], murow[:, :], ALU.mult)
            e.tt(varrow[:, :], varrow[:, :], sdrow[:, :], ALU.subtract)
            e.nop_inc(VE, 1)
            e.wait(AC, ac_at(bi, 17))
            e.recip(rstdrow[:, :], sdrow[:, :])
            e.nop_inc(VE, 1)
            e.wait(PE, pe_at(bi, 50))
            if bi >= 1:
                stR.wait(raw, st0 + bi - 1)
            for c in range(NCH):
                e.tt(t1b[:, :], retb[:, c, :], pmu[:, :], ALU.subtract)
                e.tt(t1b[:, :], t1b[:, :], prs[:, :], ALU.mult)
                e.ts(a0b[:, c, :], t1b[:, :], gam[:, c:c + 1], bet[:, c:c + 1],
                     ALU.mult, ALU.add)
            e.nop_inc(VE, 1)
        raw.drain()
        raw.sem_inc(BAR, 1)

    pctx.close()

    S.base["gp"] += 1
    S.base["pe"] += 1 + 50 * NB
    S.base["ve"] += 2 + 19 * NB
    S.base["ac"] += 1 + 17 * NB
    S.base["ev"] += 16 * NB


def phase_C(nc, S, R, TT, bar_k=0):
    name = "phC_"
    pctx = ExitStack()
    sb = lambda n, s, d=F32: pctx.enter_context(nc.sbuf_tensor(name + n, s, d))
    psal = lambda n, s, d=F32: pctx.enter_context(nc.psum_tensor(name + n, s, d))

    xn, a0f, x0, acc0 = TT["xn"], TT["a0f"], TT["x0"], TT["acc0"]

    wg_s = sb("wg", [128, 16, 2048], BF16)
    wo_s = sb("wo", [128, 16, 2048], BF16)
    xnb = sb("xnb", [128, 2, 16, 256], BF16)
    a0s = sb("a0s", [128, 16, 256], BF16)
    a1b = sb("a1b", [128, 16, 256], BF16)
    gsb = sb("gsb", [128, 2, 256])
    acct = sb("acct", [128, 2, 2048])
    x0t = sb("x0t", [128, 2, 2048])

    pgA = psal("pgA", [128, 256]); pgB = psal("pgB", [128, 256])
    pat = psal("pat", [128, 2048])

    ve0 = S.base["ve"]; pe0 = S.base["pe"]; evz = S.base["ev"]
    VE, PE, EV = S["ve"], S["pe"], S["ev"]
    ldR, wtR, stR = R["ld"], R["wt"], R["st"]
    ld0, wt0, st0 = ldR.n, wtR.n, stR.n
    BAR = S["bar"]
    barv = 5 * bar_k

    def pe_at(bi, k):
        return pe0 + 18 * bi + k

    def ve_at(bi, k):
        return ve0 + 18 * bi + k

    block = pctx.enter_context(nc.Block(name))

    @block.sync
    def _(raw):
        raw.wait_ge(BAR, barv)
        for bi in range(NB):
            if bi >= 2:
                raw.wait_ge(PE, pe_at(bi - 2, 16))
            ldR.issue(raw, xnb[:, bi % 2, :, :], xn[bi].rearrange("c d t -> d c t"))
            if bi >= 1:
                raw.wait_ge(VE, ve_at(bi - 1, 16))
            ldR.issue(raw, a0s[:, :, :], a0f[bi].rearrange("c d t -> d c t"))
            for ti in range(2):
                j = 2 * bi + ti
                if j >= 2:
                    raw.wait_ge(VE, ve_at((j - 2) // 2, 17 + (j - 2) % 2))
                ldR.issue(raw, x0t[:, j % 2, :],
                          x0[128 * j:128 * (j + 1), :])
        raw.drain()
        raw.sem_inc(BAR, 1)

    @block.gpsimd
    def _(raw):
        raw.wait_ge(BAR, barv)
        for k in range(NCH):
            wtR.issue(raw, wg_s[:, k, :], TT["wgT"][128 * k:128 * (k + 1), :])
        for k in range(NCH):
            wtR.issue(raw, wo_s[:, k, :], TT["woT"][128 * k:128 * (k + 1), :])
        raw.drain()
        raw.sem_inc(BAR, 1)

    @block.tensor
    def _(raw):
        raw.wait_ge(BAR, barv)
        for bi in range(NB):
            pb = bi % 2
            for c in range(NCH):
                if c == 0:
                    ldR.wait(raw, ld0 + 4 * bi)
                if 16 * bi + c >= 2:
                    raw.wait_ge(EV, evz + 16 * bi + c - 1)
                dst = (pgA if c % 2 == 0 else pgB)
                for k in range(NCH):
                    if bi == 0 and c == 0:
                        wtR.wait(raw, wt0 + k)
                    mmi = raw.matmul(dst[:, :], wg_s[:, k, 128 * c:128 * (c + 1)],
                                     xnb[:, pb, k, :], start=(k == 0),
                                     stop=(k == NCH - 1))
                mmi.then_inc(PE, 1)
            for ti in range(2):
                j = 2 * bi + ti
                if j >= 1:
                    raw.wait_ge(VE, ve_at((j - 1) // 2, 17 + (j - 1) % 2))
                for c in range(NCH):
                    raw.wait_ge(VE, ve_at(bi, c + 1))
                    if bi == 0 and c == 0:
                        wtR.wait_upto(raw, wt0 + 31)
                    for jj in range(4):
                        mmi = raw.matmul(pat[:, 512 * jj:512 * (jj + 1)],
                                         a1b[:, c, 128 * ti:128 * (ti + 1)],
                                         wo_s[:, c, 512 * jj:512 * (jj + 1)],
                                         start=(c == 0), stop=(c == NCH - 1))
                mmi.then_inc(PE, 1)
        raw.drain()
        raw.sem_inc(BAR, 1)

    @block.scalar
    def _(raw):
        raw.wait_ge(BAR, barv)
        e = EW(raw)
        for bi in range(NB):
            for c in range(NCH):
                e.wait(PE, pe_at(bi, c + 1))
                if c >= 2:
                    e.wait(VE, ve_at(bi, c - 1))
                e.act(gsb[:, c % 2, :], (pgA if c % 2 == 0 else pgB)[:, :],
                      AF.Silu).then_inc(EV, 1)
            for ti in range(2):
                e.wait(VE, ve_at(bi, 17 + ti))
                stR.issue(raw, acc0[128 * (2 * bi + ti):128 * (2 * bi + ti + 1), :],
                          acct[:, (2 * bi + ti) % 2, :])
        stR.wait_all(raw)
        raw.drain()
        raw.sem_inc(BAR, 1)

    @block.vector
    def _(raw):
        raw.wait_ge(BAR, barv)
        e = EW(raw)
        for bi in range(NB):
            for c in range(NCH):
                e.wait(EV, evz + 16 * bi + c + 1)
                if c == 0:
                    ldR.wait(raw, ld0 + 4 * bi + 1)
                    if bi >= 1:
                        e.wait(PE, pe_at(bi - 1, 18))
                e.tt(a1b[:, c, :], gsb[:, c % 2, :], a0s[:, c, :],
                     ALU.mult).then_inc(VE, 1)
            for ti in range(2):
                j = 2 * bi + ti
                e.wait(PE, pe_at(bi, 17 + ti))
                ldR.wait(raw, ld0 + 4 * bi + 2 + ti)
                if j >= 2:
                    stR.wait(raw, st0 + j - 2)
                for jj in range(4):
                    e.tt(acct[:, j % 2, 512 * jj:512 * (jj + 1)],
                         pat[:, 512 * jj:512 * (jj + 1)],
                         x0t[:, j % 2, 512 * jj:512 * (jj + 1)], ALU.add)
                e.nop_inc(VE, 1)
        raw.drain()
        raw.sem_inc(BAR, 1)

    pctx.close()

    S.base["pe"] += 18 * NB
    S.base["ve"] += 18 * NB
    S.base["ev"] += 16 * NB


def phase_S(nc, S, R, TT, slot, bar_k=0):
    name = f"phS{slot}_"
    pctx = ExitStack()
    sb = lambda n, s, d=F32: pctx.enter_context(nc.sbuf_tensor(name + n, s, d))
    psal = lambda n, s, d=F32: pctx.enter_context(nc.psum_tensor(name + n, s, d))

    xn, acc_in, acc_out = TT["xn"], TT["acc_in"], TT["acc_out"]
    NF = 11

    w1_s = sb("w1", [128, 16, 1408], BF16)
    w2_s = sb("w2", [128, 11, 2048], BF16)
    b1row = sb("b1row", [1, 1408], BF16)
    b2row = sb("b2row", [1, 2048], BF16)
    xnb = sb("xnb", [128, 2, 16, 256], BF16)
    hfm = sb("hfm", [128, 11, 256], BF16)
    acct = sb("acct", [128, 2, 2048])
    ri_s = sb("ri", [1, 2], I32)
    rf_s = sb("rf", [1, 8])
    ones_r = sb("ones_r", [1, 128])
    ones_rb = sb("ones_rb", [1, 128], BF16)
    ones256 = sb("ones256", [1, 256], BF16)
    wbc = sb("wbc", [128, 1])

    phA_ = psal("phA", [128, 256]); phB_ = psal("phB", [128, 256])
    peo = psal("peo", [128, 2048])
    pwb = psal("pwb", [128, 1])

    ve0 = S.base["ve"]; pe0 = S.base["pe"]; evz = S.base["ev"]; gp0 = S.base["gp"]
    VE, PE, EV, GP = S["ve"], S["pe"], S["ev"], S["gp"]
    ldR, wtR, stR = R["ld"], R["wt"], R["st"]
    ld0, wt0, st0 = ldR.n, wtR.n, stR.n
    BAR = S["bar"]
    barv = 5 * bar_k
    PEs, VEs = pe0 + 1, ve0 + 1

    def pe_at(bi, k):
        return PEs + 13 * bi + k

    def ve_at(bi, k):
        return VEs + 2 * bi + k

    block = pctx.enter_context(nc.Block(name))

    @block.sync
    def _(raw):
        raw.wait_ge(BAR, barv)
        ldR.issue(raw, ri_s[:, :], TT["route_i"])
        ldR.issue(raw, rf_s[:, :], TT["route_f"])
        for bi in range(NB):
            if bi >= 2:
                raw.wait_ge(PE, pe_at(bi - 2, 11))
            ldR.issue(raw, xnb[:, bi % 2, :, :], xn[bi].rearrange("c d t -> d c t"))
            for ti in range(2):
                j = 2 * bi + ti
                if j >= 2:
                    raw.wait_ge(VE, ve_at((j - 2) // 2, 1 + (j - 2) % 2))
                    stR.wait(raw, st0 + j - 2)
                ldR.issue(raw, acct[:, j % 2, :], acc_in[128 * j:128 * (j + 1), :])
        raw.drain()
        raw.sem_inc(BAR, 1)

    @block.gpsimd
    def _(raw):
        raw.wait_ge(BAR, barv)
        e = EW(raw)
        e.memset(ones_r[:, :], 1.0)
        e.memset(ones_rb[:, :], 1.0)
        e.memset(ones256[:, :], 1.0).then_inc(GP, 1)
        ldR.wait(raw, ld0)
        with raw.register(f"re{slot}") as re, raw.register(f"rw{slot}") as rw:
            raw.reg_load(re, ri_s[0:1, slot:slot + 1])
            raw.reg_mul(rw, re, 2048)
            rv = RuntimeValue(rw, min_val=0, max_val=5 * 2048)
            W1v = TT["W1"][ds(rv, 2048), :].rearrange("(k p) f -> p k f", p=128)
            wtR.issue(raw, w1_s[:, 0:8, :], W1v[:, 0:8, :])
            wtR.issue(raw, w1_s[:, 8:16, :], W1v[:, 8:16, :])
            rv = RuntimeValue(re, min_val=0, max_val=5)
            wtR.issue(raw, b1row[:, :], TT["b1"][ds(rv, 1), :])
            raw.reg_mul(rw, re, 1408)
            rv = RuntimeValue(rw, min_val=0, max_val=5 * 1408)
            W2v = TT["W2"][ds(rv, 1408), :].rearrange("(k p) f -> p k f", p=128)
            wtR.issue(raw, w2_s[:, 0:6, :], W2v[:, 0:6, :])
            wtR.issue(raw, w2_s[:, 6:11, :], W2v[:, 6:11, :])
            rv = RuntimeValue(re, min_val=0, max_val=5)
            wtR.issue(raw, b2row[:, 0:1024], TT["b2"][ds(rv, 1), 0:1024])
            wtR.issue(raw, b2row[:, 1024:2048], TT["b2"][ds(rv, 1), 1024:2048])
        raw.drain()
        raw.sem_inc(BAR, 1)

    @block.tensor
    def _(raw):
        raw.wait_ge(BAR, barv)
        ldR.wait(raw, ld0 + 1)
        raw.wait_ge(GP, gp0 + 1)
        raw.matmul(pwb[:, :], ones_r[0:1, :], rf_s[0:1, 2 + slot:3 + slot],
                   start=True, stop=True).then_inc(PE, 1)
        for bi in range(NB):
            pb = bi % 2
            for fc in range(NF):
                if fc == 0:
                    ldR.wait(raw, ld0 + 2 + 3 * bi)
                gidx = 11 * bi + fc
                if gidx >= 2:
                    raw.wait_ge(EV, evz + gidx - 1)
                dst = (phA_ if fc % 2 == 0 else phB_)
                if bi == 0 and fc == 0:
                    wtR.wait_upto(raw, wt0 + 6)
                raw.matmul(dst[:, :], b1row[0:1, 128 * fc:128 * (fc + 1)],
                           ones256[0:1, :], start=True, stop=False)
                for k in range(NCH):
                    mmi = raw.matmul(dst[:, :], w1_s[:, k, 128 * fc:128 * (fc + 1)],
                                     xnb[:, pb, k, :], start=False,
                                     stop=(k == NCH - 1))
                mmi.then_inc(PE, 1)
            for ti in range(2):
                j = 2 * bi + ti
                if j >= 1:
                    raw.wait_ge(VE, ve_at((j - 1) // 2, 1 + (j - 1) % 2))
                for jj in range(4):
                    raw.matmul(peo[:, 512 * jj:512 * (jj + 1)], ones_rb[0:1, :],
                               b2row[0:1, 512 * jj:512 * (jj + 1)],
                               start=True, stop=False)
                for fc in range(NF):
                    raw.wait_ge(EV, evz + 11 * bi + fc + 1)
                    for jj in range(4):
                        mmi = raw.matmul(peo[:, 512 * jj:512 * (jj + 1)],
                                         hfm[:, fc, 128 * ti:128 * (ti + 1)],
                                         w2_s[:, fc, 512 * jj:512 * (jj + 1)],
                                         start=False, stop=(fc == NF - 1))
                mmi.then_inc(PE, 1)
        raw.drain()
        raw.sem_inc(BAR, 1)

    @block.scalar
    def _(raw):
        raw.wait_ge(BAR, barv)
        e = EW(raw)
        for bi in range(NB):
            for fc in range(NF):
                e.wait(PE, pe_at(bi, fc + 1))
                if bi >= 1 and fc == 0:
                    e.wait(PE, pe_at(bi - 1, 13))
                e.act(hfm[:, fc, :], (phA_ if fc % 2 == 0 else phB_)[:, :],
                      AF.Gelu).then_inc(EV, 1)
            for ti in range(2):
                j = 2 * bi + ti
                e.wait(VE, ve_at(bi, 1 + ti))
                stR.issue(raw, acc_out[128 * j:128 * (j + 1), :],
                          acct[:, j % 2, :])
        stR.wait_all(raw)
        raw.drain()
        raw.sem_inc(BAR, 1)

    @block.vector
    def _(raw):
        raw.wait_ge(BAR, barv)
        e = EW(raw)
        e.wait(PE, PEs)
        e.copy(wbc[:, :], pwb[:, :])
        e.nop_inc(VE, 1)
        for bi in range(NB):
            for ti in range(2):
                j = 2 * bi + ti
                e.wait(PE, pe_at(bi, 12 + ti))
                ldR.wait(raw, ld0 + 2 + 3 * bi + 1 + ti)
                if j >= 2:
                    stR.wait(raw, st0 + j - 2)
                for jj in range(4):
                    e.stt(acct[:, j % 2, 512 * jj:512 * (jj + 1)],
                          peo[:, 512 * jj:512 * (jj + 1)], wbc[:, 0:1],
                          acct[:, j % 2, 512 * jj:512 * (jj + 1)],
                          ALU.mult, ALU.add)
                e.nop_inc(VE, 1)
        raw.drain()
        raw.sem_inc(BAR, 1)

    pctx.close()

    S.base["gp"] += 1
    S.base["pe"] += 1 + 13 * NB
    S.base["ve"] += 1 + 2 * NB
    S.base["ev"] += 11 * NB


def phase_F1(nc, S, R, TT, bar_k=0):
    name = "phF1_"
    pctx = ExitStack()
    sb = lambda n, s, d=F32: pctx.enter_context(nc.sbuf_tensor(name + n, s, d))
    psal = lambda n, s, d=F32: pctx.enter_context(nc.psum_tensor(name + n, s, d))

    xn, rkvf, rksf = TT["xn"], TT["rkvf"], TT["rksf"]

    wk_s = sb("wk", [128, 16, 2048], BF16)
    wv_s = sb("wv", [128, 16, 2048], BF16)
    xnb = sb("xnb", [128, 2, 16, 256], BF16)
    relb = sb("relb", [128, 2, 256])
    mnb = sb("mnb", [128, 2, 256])
    enb = sb("enb", [128, 2, 256])
    phikb = sb("phikb", [128, 16, 256], BF16)
    kvb = sb("kvb", [128, 16, 256], BF16)
    rkvb = sb("rkvb", [128, 16, 256])
    rksb = sb("rksb", [128, 16, 256])
    lam_col = sb("lam", [128, 16])
    state2 = sb("state2", [128, 32])
    drow = sb("drow", [1, 16])
    ones_r = sb("ones_r", [1, 128])

    pkA = psal("pkA", [128, 256]); pkB = psal("pkB", [128, 256])
    pvA = psal("pvA", [128, 256]); pvB = psal("pvB", [128, 256])
    plam = psal("plam", [128, 16])

    ve0 = S.base["ve"]; ac0 = S.base["ac"]; pe0 = S.base["pe"]
    gp0 = S.base["gp"]; evz = S.base["ev"]
    VE, AC, PE, GP, EV = S["ve"], S["ac"], S["pe"], S["gp"], S["ev"]
    ldR, wtR, stR = R["ld"], R["wt"], R["st"]
    ld0, wt0, st0 = ldR.n, wtR.n, stR.n
    BAR = S["bar"]
    barv = 5 * bar_k
    PEs, VEs, ACs = pe0 + 1, ve0 + 2, ac0 + 1

    def pe_at(bi, k):
        return PEs + 32 * bi + k

    def ve_at(bi, k):
        return VEs + 33 * bi + k

    def ac_at(bi, k):
        return ACs + 32 * bi + k

    block = pctx.enter_context(nc.Block(name))

    @block.sync
    def _(raw):
        raw.wait_ge(BAR, barv)
        ldR.issue(raw, drow[:, :], TT["decay1"])
        for bi in range(NB):
            if bi >= 2:
                raw.wait_ge(PE, pe_at(bi - 2, 32))
            ldR.issue(raw, xnb[:, bi % 2, :, :], xn[bi].rearrange("c d t -> d c t"))
        raw.drain()
        raw.sem_inc(BAR, 1)

    @block.gpsimd
    def _(raw):
        raw.wait_ge(BAR, barv)
        e = EW(raw)
        e.memset(ones_r[:, :], 1.0)
        e.memset(state2[:, :], 0.0).then_inc(GP, 1)
        for k in range(NCH):
            wtR.issue(raw, wk_s[:, k, :],
                      TT["w_qkv"][128 * k:128 * (k + 1), 2048:4096])
        for k in range(NCH):
            wtR.issue(raw, wv_s[:, k, :],
                      TT["w_qkv"][128 * k:128 * (k + 1), 4096:6144])
        for bi in range(NB):
            raw.wait_ge(VE, ve_at(bi, 33))
            wtR.issue(raw, rkvf[bi].rearrange("c d t -> d c t"), rkvb[:, :, :])
            wtR.issue(raw, rksf[bi].rearrange("c d t -> d c t"), rksb[:, :, :])
        wtR.wait_all(raw)
        raw.drain()
        raw.sem_inc(BAR, 1)

    @block.scalar
    def _(raw):
        raw.wait_ge(BAR, barv)
        e = EW(raw)
        ldR.wait(raw, ld0)
        e.act(drow[:, :], drow[:, :], AF.Sigmoid).then_inc(AC, 1)
        for bi in range(NB):
            for c in range(NCH):
                p = c % 2
                e.wait(PE, pe_at(bi, 2 * c + 1))
                e.act(relb[:, p, :], (pkA if p == 0 else pkB)[:, :],
                      AF.Relu).then_inc(AC, 1)
                e.wait(VE, ve_at(bi, 2 * c + 1))
                e.act(enb[:, p, :], mnb[:, p, :], AF.Exp).then_inc(AC, 1)
        raw.drain()
        raw.sem_inc(BAR, 1)

    @block.tensor
    def _(raw):
        raw.wait_ge(BAR, barv)
        raw.wait_ge(AC, ACs)
        raw.wait_ge(GP, gp0 + 1)
        raw.matmul(plam[:, :], ones_r[0:1, :], drow[0:1, :],
                   start=True, stop=True).then_inc(PE, 1)
        for bi in range(NB):
            pb = bi % 2
            for c in range(NCH):
                if c == 0:
                    ldR.wait(raw, ld0 + 1 + bi)
                p = c % 2
                if 16 * bi + c >= 2:
                    raw.wait_ge(AC, ac_at(bi, 2 * (c - 2) + 1))
                    raw.wait_ge(VE, ve_at(bi, 2 * (c - 2) + 1))
                dstk = (pkA if p == 0 else pkB)
                for k in range(NCH):
                    if bi == 0 and c == 0:
                        wtR.wait(raw, wt0 + k)
                    mmi = raw.matmul(dstk[:, :], wk_s[:, k, 128 * c:128 * (c + 1)],
                                     xnb[:, pb, k, :], start=(k == 0),
                                     stop=(k == NCH - 1))
                mmi.then_inc(PE, 1)
                if 16 * bi + c >= 2:
                    raw.wait_ge(VE, ve_at(bi, 2 * (c - 2) + 2))
                dstv = (pvA if p == 0 else pvB)
                for k in range(NCH):
                    if bi == 0 and c == 0:
                        wtR.wait(raw, wt0 + 16 + k)
                    mmi = raw.matmul(dstv[:, :], wv_s[:, k, 128 * c:128 * (c + 1)],
                                     xnb[:, pb, k, :], start=(k == 0),
                                     stop=(k == NCH - 1))
                mmi.then_inc(PE, 1)
        raw.drain()
        raw.sem_inc(BAR, 1)

    @block.vector
    def _(raw):
        raw.wait_ge(BAR, barv)
        e = EW(raw)
        e.wait(PE, PEs)
        e.copy(lam_col[:, :], plam[:, :])
        e.nop_inc(VE, 1)
        e.nop_inc(VE, 1)
        for bi in range(NB):
            if bi >= 1:
                wtR.wait(raw, wt0 + 32 + 2 * (bi - 1))
                wtR.wait(raw, wt0 + 32 + 2 * (bi - 1) + 1)
            for c in range(NCH):
                p = c % 2
                e.wait(PE, pe_at(bi, 2 * c + 1))
                e.wait(AC, ac_at(bi, 2 * c + 1))
                e.tt(mnb[:, p, :], (pkA if p == 0 else pkB)[:, :],
                     relb[:, p, :], ALU.subtract).then_inc(VE, 1)
                e.wait(AC, ac_at(bi, 2 * c + 2))
                e.tt(phikb[:, c, :], enb[:, p, :], relb[:, p, :], ALU.add)
                e.wait(PE, pe_at(bi, 2 * c + 2))
                e.tt(kvb[:, c, :], (pvA if p == 0 else pvB)[:, :],
                     phikb[:, c, :], ALU.mult).then_inc(VE, 1)
            for h in range(H):
                lam_b = bass.AP(lam_col, h, [[16, 128], [0, TB]])
                e.scan(rkvb[:, h, :], lam_b, kvb[:, h, :], state2[:, h:h + 1])
                e.scan(rksb[:, h, :], lam_b, phikb[:, h, :],
                       state2[:, 16 + h:16 + h + 1])
            e.copy(state2[:, 0:16], rkvb[:, :, TB - 1:TB])
            e.copy(state2[:, 16:32], rksb[:, :, TB - 1:TB])
            e.nop_inc(VE, 1)
        raw.drain()
        raw.sem_inc(BAR, 1)

    pctx.close()

    S.base["gp"] += 1
    S.base["pe"] += 1 + 32 * NB
    S.base["ve"] += 2 + 33 * NB
    S.base["ac"] += 1 + 32 * NB


def phase_F2(nc, S, R, TT, bar_k=0):
    name = "phF2_"
    pctx = ExitStack()
    sb = lambda n, s, d=F32: pctx.enter_context(nc.sbuf_tensor(name + n, s, d))
    psal = lambda n, s, d=F32: pctx.enter_context(nc.psum_tensor(name + n, s, d))

    xn, rkvf, rksf, x1, acc1 = (TT["xn"], TT["rkvf"], TT["rksf"], TT["x1"],
                                TT["acc1"])
    den_hbm = TT["den_hbm"]

    wq_s = sb("wq", [128, 16, 2048], BF16)
    wo_s = sb("wo", [128, 16, 2048], BF16)
    xnb = sb("xnb", [128, 2, 16, 256], BF16)
    rkvs = sb("rkvs", [128, 16, 256], BF16)
    rkss = sb("rkss", [128, 16, 256], BF16)
    relb = sb("relb", [128, 2, 256])
    mnb = sb("mnb", [128, 2, 256])
    enb = sb("enb", [128, 2, 256])
    phiq = sb("phiq", [128, 256])
    zb = sb("zb", [128, 16, 256], BF16)
    y2b = sb("y2b", [128, 16, 256], BF16)
    ones_cb = sb("ones_cb", [128, 1], BF16)
    drow = sb("drow", [1, 256])
    rdrow = sb("rdrow", [1, 256])
    denc = sb("denc", [128, 2, 1])
    x1t = sb("x1t", [128, 2, 2048])

    pqA = psal("pqA", [128, 256]); pqB = psal("pqB", [128, 256])
    pdn = psal("pdn", [1, 256])
    pat = psal("pat", [128, 2048])

    ve0 = S.base["ve"]; ac0 = S.base["ac"]; pe0 = S.base["pe"]
    gp0 = S.base["gp"]; evz = S.base["ev"]
    VE, AC, PE, GP, EV = S["ve"], S["ac"], S["pe"], S["gp"], S["ev"]
    ldR, wtR, stR, x2R = R["ld"], R["wt"], R["st"], R["x2"]
    ld0, wt0, st0, x20 = ldR.n, wtR.n, stR.n, x2R.n
    BAR = S["bar"]
    barv = 5 * bar_k

    def pe_at(bi, k):
        return pe0 + 34 * bi + k

    def ve_at(bi, k):
        return ve0 + 35 * bi + k

    def ac_at(bi, k):
        return ac0 + 32 * bi + k

    block = pctx.enter_context(nc.Block(name))

    @block.sync
    def _(raw):
        raw.wait_ge(BAR, barv)
        for bi in range(NB):
            if bi >= 2:
                raw.wait_ge(PE, pe_at(bi - 2, 16))
            ldR.issue(raw, xnb[:, bi % 2, :, :], xn[bi].rearrange("c d t -> d c t"))
            if bi >= 1:
                raw.wait_ge(VE, ve_at(bi - 1, 32))
            ldR.issue(raw, rkvs[:, :, :], rkvf[bi].rearrange("c d t -> d c t"))
            ldR.issue(raw, rkss[:, :, :], rksf[bi].rearrange("c d t -> d c t"))
            for ti in range(2):
                j = 2 * bi + ti
                if j >= 2:
                    raw.wait_ge(VE, ve_at((j - 2) // 2, 34 + (j - 2) % 2))
                    stR.wait(raw, st0 + j - 2)
                ldR.issue(raw, x1t[:, j % 2, :], x1[128 * j:128 * (j + 1), :])
            x2R.wait(raw, x20 + bi)
            for ti in range(2):
                j = 2 * bi + ti
                ldR.issue(raw, denc[:, j % 2, :],
                          den_hbm[0, 128 * j:128 * (j + 1)].rearrange(
                              "(p o) -> p o", o=1),
                          allow_slow_non_contiguous=True)
        raw.drain()
        raw.sem_inc(BAR, 1)

    @block.gpsimd
    def _(raw):
        raw.wait_ge(BAR, barv)
        e = EW(raw)
        e.memset(ones_cb[:, :], 1.0).then_inc(GP, 1)
        for k in range(NCH):
            wtR.issue(raw, wq_s[:, k, :], TT["w_qkv"][128 * k:128 * (k + 1), 0:2048])
        for k in range(NCH):
            wtR.issue(raw, wo_s[:, k, :], TT["w_out"][128 * k:128 * (k + 1), :])
        raw.drain()
        raw.sem_inc(BAR, 1)

    @block.scalar
    def _(raw):
        raw.wait_ge(BAR, barv)
        e = EW(raw)
        for bi in range(NB):
            for c in range(NCH):
                p = c % 2
                e.wait(PE, pe_at(bi, c + 1))
                e.act(relb[:, p, :], (pqA if p == 0 else pqB)[:, :],
                      AF.Relu).then_inc(AC, 1)
                e.wait(VE, ve_at(bi, 2 * c + 1))
                e.act(enb[:, p, :], mnb[:, p, :], AF.Exp).then_inc(AC, 1)
            e.wait(VE, ve_at(bi, 33))
            x2R.issue(raw, den_hbm[0:1, 256 * bi:256 * (bi + 1)], rdrow[:, :])
            for ti in range(2):
                j = 2 * bi + ti
                e.wait(VE, ve_at(bi, 34 + ti))
                stR.issue(raw, acc1[128 * j:128 * (j + 1), :], x1t[:, j % 2, :])
        stR.wait_all(raw)
        raw.drain()
        raw.sem_inc(BAR, 1)

    @block.tensor
    def _(raw):
        raw.wait_ge(BAR, barv)
        for bi in range(NB):
            pb = bi % 2
            for c in range(NCH):
                if c == 0:
                    ldR.wait(raw, ld0 + 7 * bi)
                p = c % 2
                if 16 * bi + c >= 2:
                    raw.wait_ge(AC, ac_at(bi, 2 * (c - 2) + 1))
                    raw.wait_ge(VE, ve_at(bi, 2 * (c - 2) + 1))
                dst = (pqA if p == 0 else pqB)
                for k in range(NCH):
                    if bi == 0 and c == 0:
                        wtR.wait(raw, wt0 + k)
                    mmi = raw.matmul(dst[:, :], wq_s[:, k, 128 * c:128 * (c + 1)],
                                     xnb[:, pb, k, :], start=(k == 0),
                                     stop=(k == NCH - 1))
                mmi.then_inc(PE, 1)
            if bi >= 1:
                raw.wait_ge(VE, ve_at(bi - 1, 33))
            raw.wait_ge(GP, gp0 + 1)
            for c in range(NCH):
                raw.wait_ge(VE, ve_at(bi, 2 * c + 2))
                mmi = raw.matmul(pdn[0:1, :], ones_cb[:, 0:1], y2b[:, c, :],
                                 start=(c == 0), stop=(c == NCH - 1))
                mmi.then_inc(PE, 1)
            for ti in range(2):
                j = 2 * bi + ti
                if j >= 1:
                    raw.wait_ge(VE, ve_at((j - 1) // 2, 34 + (j - 1) % 2))
                for c in range(NCH):
                    raw.wait_ge(VE, ve_at(bi, 2 * c + 2))
                    if bi == 0 and c == 0:
                        wtR.wait_upto(raw, wt0 + 31)
                    for jj in range(4):
                        mmi = raw.matmul(pat[:, 512 * jj:512 * (jj + 1)],
                                         zb[:, c, 128 * ti:128 * (ti + 1)],
                                         wo_s[:, c, 512 * jj:512 * (jj + 1)],
                                         start=(c == 0), stop=(c == NCH - 1))
                mmi.then_inc(PE, 1)
        raw.drain()
        raw.sem_inc(BAR, 1)

    @block.vector
    def _(raw):
        raw.wait_ge(BAR, barv)
        e = EW(raw)
        for bi in range(NB):
            for c in range(NCH):
                p = c % 2
                e.wait(PE, pe_at(bi, c + 1))
                e.wait(AC, ac_at(bi, 2 * c + 1))
                e.tt(mnb[:, p, :], (pqA if p == 0 else pqB)[:, :],
                     relb[:, p, :], ALU.subtract).then_inc(VE, 1)
                e.wait(AC, ac_at(bi, 2 * c + 2))
                e.tt(phiq[:, :], enb[:, p, :], relb[:, p, :], ALU.add)
                if c == 0:
                    ldR.wait(raw, ld0 + 7 * bi + 1)
                    ldR.wait(raw, ld0 + 7 * bi + 2)
                if bi >= 1 and c == 0:
                    raw.wait_ge(PE, pe_at(bi - 1, 34))
                e.tt(zb[:, c, :], phiq[:, :], rkvs[:, c, :], ALU.mult)
                e.tt(y2b[:, c, :], phiq[:, :], rkss[:, c, :],
                     ALU.mult).then_inc(VE, 1)
            e.wait(PE, pe_at(bi, 32))
            e.ts(drow[:, :], pdn[0:1, :], 1e-6, None, ALU.max)
            e.recip(rdrow[:, :], drow[:, :])
            e.nop_inc(VE, 1)
            for ti in range(2):
                j = 2 * bi + ti
                e.wait(PE, pe_at(bi, 33 + ti))
                ldR.wait(raw, ld0 + 7 * bi + 3 + ti)
                ldR.wait(raw, ld0 + 7 * bi + 5 + ti)
                for jj in range(4):
                    e.stt(x1t[:, j % 2, 512 * jj:512 * (jj + 1)],
                          pat[:, 512 * jj:512 * (jj + 1)], denc[:, j % 2, 0:1],
                          x1t[:, j % 2, 512 * jj:512 * (jj + 1)],
                          ALU.mult, ALU.add)
                e.nop_inc(VE, 1)
        raw.drain()
        raw.sem_inc(BAR, 1)

    pctx.close()

    S.base["gp"] += 1
    S.base["pe"] += 34 * NB
    S.base["ve"] += 35 * NB
    S.base["ac"] += 32 * NB


def phase_G(nc, S, R, TT, bar_k=0):
    name = "phG_"
    pctx = ExitStack()
    sb = lambda n, s, d=F32: pctx.enter_context(nc.sbuf_tensor(name + n, s, d))
    psal = lambda n, s, d=F32: pctx.enter_context(nc.psum_tensor(name + n, s, d))

    xn, comb, acc1, out = TT["xn"], TT["comb"], TT["acc1"], TT["out"]
    w1c, w3c, w2c = TT["w1c"], TT["w3c"], TT["w2c"]
    NS = 4
    SPB = 4
    NU = 9

    w1u = sb("w1u", [128, 16, 1024], BF16)
    w3u = sb("w3u", [128, 16, 1024], BF16)
    w2u = sb("w2u", [128, 8, 2048], BF16)
    xns = sb("xns", [128, SPB, 16, 256], BF16)
    accs = sb("accs", [128, 8, 2048])
    gfm = sb("gfm", [128, 8, 256], BF16)
    hs = sb("hs", [128, 2, 256], BF16)
    combs = sb("combs", [128, 8, 8])

    phA_ = psal("phA", [128, 256]); phB_ = psal("phB", [128, 256])
    p3A = psal("p3A", [128, 256]); p3B = psal("p3B", [128, 256])
    eoA = psal("eoA", [128, 512]); eoB = psal("eoB", [128, 512])

    ve0 = S.base["ve"]; ac0 = S.base["ac"]; pe0 = S.base["pe"]
    VE, AC, PE = S["ve"], S["ac"], S["pe"]
    ldR, wtR, stR = R["ld"], R["wt"], R["st"]
    ld0, wt0, st0 = ldR.n, wtR.n, stR.n
    BAR = S["bar"]
    barv = 5 * bar_k

    def pe_at(s, u, k):
        return pe0 + 864 * s + 96 * u + k

    def ve_at(s, u, k):
        return ve0 + 576 * s + 64 * u + k

    def ac_at(s, u, k):
        return ac0 + 288 * s + 32 * u + k

    def ld_id(s, u=None, w=0):
        base = ld0 + 93 * s
        if u is None:
            return base
        return base + 12 + 9 * u + w

    CAST = 27

    def wt_acc_id(s, t):
        return wt0 + CAST + 8 * s + t

    def prev_unit(s, u):
        return (s, u - 1) if u >= 1 else (s - 1, 8)

    block = pctx.enter_context(nc.Block(name))

    @block.gpsimd
    def _(raw):
        raw.wait_ge(BAR, barv)
        for u in range(NU):
            wtR.issue(raw, w1c[2048 * u:2048 * (u + 1), :],
                      TT["w1a"][2048 * u:2048 * (u + 1), :])
            wtR.issue(raw, w3c[2048 * u:2048 * (u + 1), :],
                      TT["w3a"][2048 * u:2048 * (u + 1), :])
            wtR.issue(raw, w2c[1024 * u:1024 * (u + 1), :],
                      TT["w2a"][1024 * u:1024 * (u + 1), :])
        for s in range(NS):
            for t in range(8):
                j = 8 * s + t
                raw.wait_ge(VE, ve_at(s, 8, 16 * (t // 2) + 8 + 4 * (t % 2) + 4))
                wtR.issue(raw, accs[:, t, :], acc1[128 * j:128 * (j + 1), :],
                          accum_op=ALU.add)
        raw.drain()
        raw.sem_inc(BAR, 1)

    @block.sync
    def _(raw):
        raw.wait_ge(BAR, barv)
        for s in range(NS):
            if s >= 1:
                raw.wait_ge(PE, pe_at(s - 1, 8, 88))
            for b in range(SPB):
                ldR.issue(raw, xns[:, b, :, :],
                          xn[SPB * s + b].rearrange("c d t -> d c t"))
            for t in range(8):
                j = 8 * s + t
                if s >= 1:
                    raw.wait_ge(VE, ve_at(s - 1, 8, 16 * (t // 2) + 8 + 4 * (t % 2) + 4))
                ldR.issue(raw, combs[:, t, :], comb[128 * j:128 * (j + 1), :])
            for u in range(NU):
                if 9 * s + u >= 1:
                    sp, up = prev_unit(s, u)
                    raw.wait_ge(PE, pe_at(sp, up, 88))
                if s == 0:
                    wtR.wait(raw, wt0 + 3 * u)
                    wtR.wait(raw, wt0 + 3 * u + 1)
                w1src = w1c[2048 * u:2048 * (u + 1), :].rearrange(
                    "(k p) f -> p k f", p=128)
                w3src = w3c[2048 * u:2048 * (u + 1), :].rearrange(
                    "(k p) f -> p k f", p=128)
                for q in range(4):
                    f0, f1 = 256 * q, 256 * (q + 1)
                    ldR.issue(raw, w1u[:, :, f0:f1], w1src[:, :, f0:f1])
                    ldR.issue(raw, w3u[:, :, f0:f1], w3src[:, :, f0:f1])
                if 9 * s + u >= 1:
                    sp, up = prev_unit(s, u)
                    raw.wait_ge(PE, pe_at(sp, up, 96))
                if s == 0:
                    wtR.wait(raw, wt0 + 3 * u + 2)
                ldR.issue(raw, w2u[:, :, :],
                          w2c[1024 * u:1024 * (u + 1), :].rearrange(
                              "(k p) f -> p k f", p=128))
        raw.drain()
        raw.sem_inc(BAR, 1)

    @block.tensor
    def _(raw):
        raw.wait_ge(BAR, barv)
        for s in range(NS):
            for u in range(NU):
                if u == 0:
                    for b in range(SPB):
                        ldR.wait(raw, ld_id(s) + b)
                ldR.wait(raw, ld_id(s, u, 0))
                ldR.wait(raw, ld_id(s, u, 1))
                for b in range(SPB):
                    for i in range(8):
                        ci = 8 * b + i
                        if b == 0 and i in (2, 4, 6):
                            ldR.wait(raw, ld_id(s, u, i))
                            ldR.wait(raw, ld_id(s, u, i + 1))
                        dst = (phA_ if i % 2 == 0 else phB_)
                        if ci >= 2:
                            bb, ii = (ci - 2) // 8, (ci - 2) % 8
                            raw.wait_ge(AC, ac_at(s, u, 8 * bb + ii + 1))
                        elif 9 * s + u >= 1:
                            sp, up = prev_unit(s, u)
                            raw.wait_ge(AC, ac_at(sp, up, 31 + i))
                        for k in range(NCH):
                            mmi = raw.matmul(dst[:, :],
                                             w1u[:, k, 128 * i:128 * (i + 1)],
                                             xns[:, b, k, :], start=(k == 0),
                                             stop=(k == NCH - 1))
                        mmi.then_inc(PE, 1)
                        d3 = (p3A if i % 2 == 0 else p3B)
                        if ci >= 2:
                            bb, ii = (ci - 2) // 8, (ci - 2) % 8
                            raw.wait_ge(VE, ve_at(s, u, 16 * bb + ii + 1))
                        elif 9 * s + u >= 1:
                            sp, up = prev_unit(s, u)
                            raw.wait_ge(VE, ve_at(sp, up, 48 + 6 + i + 1))
                        for k in range(NCH):
                            mmi = raw.matmul(d3[:, :],
                                             w3u[:, k, 128 * i:128 * (i + 1)],
                                             xns[:, b, k, :], start=(k == 0),
                                             stop=(k == NCH - 1))
                        mmi.then_inc(PE, 1)
                    if b == 0:
                        ldR.wait(raw, ld_id(s, u, 8))
                    for ti in range(2):
                        for jj in range(4):
                            eidx = 4 * ti + jj
                            dst = (eoA if eidx % 2 == 0 else eoB)
                            gi = 8 * b + eidx
                            if gi >= 2:
                                bb, ee = (gi - 2) // 8, (gi - 2) % 8
                                raw.wait_ge(VE, ve_at(s, u, 16 * bb + 8 + ee + 1))
                            elif 9 * s + u >= 1:
                                sp, up = prev_unit(s, u)
                                raw.wait_ge(VE, ve_at(sp, up, 48 + 8 + 6 + eidx + 1))
                            for i in range(8):
                                if eidx == 0:
                                    raw.wait_ge(VE, ve_at(s, u, 16 * b + i + 1))
                                mmi = raw.matmul(
                                    dst[:, :], gfm[:, i, 128 * ti:128 * (ti + 1)],
                                    w2u[:, i, 512 * jj:512 * (jj + 1)],
                                    start=(i == 0), stop=(i == 7))
                            mmi.then_inc(PE, 1)
        raw.drain()
        raw.sem_inc(BAR, 1)

    @block.scalar
    def _(raw):
        raw.wait_ge(BAR, barv)
        e = EW(raw)
        for s in range(NS):
            for u in range(NU):
                for b in range(SPB):
                    for i in range(8):
                        ci = 8 * b + i
                        e.wait(PE, pe_at(s, u, 24 * b + 2 * i + 1))
                        if ci >= 2:
                            bb, ii = (ci - 2) // 8, (ci - 2) % 8
                            e.wait(VE, ve_at(s, u, 16 * bb + ii + 1))
                        elif 9 * s + u >= 1:
                            sp, up = prev_unit(s, u)
                            e.wait(VE, ve_at(sp, up, 48 + 6 + i + 1))
                        e.act(hs[:, i % 2, :],
                              (phA_ if i % 2 == 0 else phB_)[:, :],
                              AF.Silu).then_inc(AC, 1)
            for t in range(8):
                j = 8 * s + t
                wtR.wait(raw, wt_acc_id(s, t))
                stR.issue(raw, out[128 * j:128 * (j + 1), :], accs[:, t, :])
        stR.wait_all(raw)
        raw.drain()
        raw.sem_inc(BAR, 1)

    @block.vector
    def _(raw):
        raw.wait_ge(BAR, barv)
        e = EW(raw)
        for s in range(NS):
            for u in range(NU):
                for b in range(SPB):
                    for i in range(8):
                        e.wait(PE, pe_at(s, u, 24 * b + 2 * i + 2))
                        e.wait(AC, ac_at(s, u, 8 * b + i + 1))
                        if u == 0 and b == 0 and i == 0 and s >= 1:
                            for t in range(8):
                                stR.wait(raw, st0 + 8 * (s - 1) + t)
                        e.tt(gfm[:, i, :], (p3A if i % 2 == 0 else p3B)[:, :],
                             hs[:, i % 2, :], ALU.mult).then_inc(VE, 1)
                    for ti in range(2):
                        t = 2 * b + ti
                        for jj in range(4):
                            eidx = 4 * ti + jj
                            e.wait(PE, pe_at(s, u, 24 * b + 16 + eidx + 1))
                            dst = (eoA if eidx % 2 == 0 else eoB)
                            sl = accs[:, t, 512 * jj:512 * (jj + 1)]
                            if u == 0:
                                e.copy(sl, dst[:, :]).then_inc(VE, 1)
                            else:
                                if u == 1 and jj == 0:
                                    ldR.wait(raw, ld_id(s) + 4 + t)
                                e.stt(sl, dst[:, :], combs[:, t, u - 1:u],
                                      accs[:, t, 512 * jj:512 * (jj + 1)],
                                      ALU.mult, ALU.add).then_inc(VE, 1)
        raw.drain()
        raw.sem_inc(BAR, 1)

    pctx.close()

    S.base["pe"] += 864 * NS
    S.base["ve"] += 576 * NS
    S.base["ac"] += 288 * NS


def _prep_inputs(inputs):
    P = {}
    P["x"] = np.ascontiguousarray(inputs["L0_attn_x"], np.float32)
    P["wqT"] = np.ascontiguousarray(inputs["L0_attn_Wq"].T)
    P["wvT"] = np.ascontiguousarray(inputs["L0_attn_Wv"].T)
    P["wgT"] = np.ascontiguousarray(inputs["L0_attn_Wg"].T)
    P["woT"] = np.ascontiguousarray(inputs["L0_attn_Wo"].T)
    g = np.asarray(inputs["L0_attn_ln_gamma"], np.float32)
    b = np.asarray(inputs["L0_attn_ln_beta"], np.float32)
    P["gamma_l"] = np.ascontiguousarray(g.reshape(16, 128).T)
    P["beta_l"] = np.ascontiguousarray(b.reshape(16, 128).T)
    P["beta0"] = np.asarray(inputs["L0_attn_beta"], np.float32).reshape(1, 16)
    gw0 = np.asarray(inputs["L0_ffn_gate_W"], np.float32)
    P["gw0_l"] = np.ascontiguousarray(
        gw0.T.reshape(16, 128, 6).transpose(1, 0, 2).reshape(128, 96))
    P["W1"] = np.ascontiguousarray(inputs["L0_ffn_W1"], np.float32)
    P["b1"] = np.ascontiguousarray(inputs["L0_ffn_b1"], np.float32)
    P["W2"] = np.ascontiguousarray(inputs["L0_ffn_W2"], np.float32)
    P["b2"] = np.ascontiguousarray(inputs["L0_ffn_b2"], np.float32)
    P["w_qkv"] = np.ascontiguousarray(inputs["L1_attn_w_qkv"], np.float32)
    P["w_out"] = np.ascontiguousarray(inputs["L1_attn_w_out"], np.float32)
    P["decay1"] = np.asarray(inputs["L1_attn_decay_param"], np.float32).reshape(1, 16)
    gw1 = np.asarray(inputs["L1_ffn_gate_w"], np.float32)
    P["gw1_l"] = np.ascontiguousarray(
        gw1.T.reshape(16, 128, 8).transpose(1, 0, 2).reshape(128, 128))
    P["w1T"] = np.ascontiguousarray(inputs["L1_ffn_w1"].transpose(0, 2, 1))
    P["w3T"] = np.ascontiguousarray(inputs["L1_ffn_w3"].transpose(0, 2, 1))
    P["w2T"] = np.ascontiguousarray(inputs["L1_ffn_w2"].transpose(0, 2, 1))
    P["ws1T"] = np.ascontiguousarray(inputs["L1_ffn_ws1"].T)
    P["ws3T"] = np.ascontiguousarray(inputs["L1_ffn_ws3"].T)
    P["ws2T"] = np.ascontiguousarray(inputs["L1_ffn_ws2"].T)
    return P


FM_SHAPE = [NB, NCH, 128, TB]


def _build_one(phase_fn, specs, extra=None):
    nc = bass.Bass("TRN2", target_bir_lowering=False, debug=False)
    TT = {}
    outs = []
    for key, shape, dt, kind in specs:
        t = nc.dram_tensor(key, shape, dt, kind=kind)
        TT[key] = t.ap()
        if kind == "ExternalOutput":
            outs.append(key)
    with ExitStack() as ctx:
        S = Sems(nc, ctx)
        R = {"ld": DmaRing(nc, ctx, "ld", 2), "wt": DmaRing(nc, ctx, "wt", 4),
             "st": DmaRing(nc, ctx, "st", 2), "x2": DmaRing(nc, ctx, "x2", 2)}
        phase_fn(nc, S, R, TT)
    return nc, outs


_PHASES = None


def _phase_defs():
    EI, EO, IN = "ExternalInput", "ExternalOutput", "Internal"
    return [
        ("A", lambda nc, S, R, TT: phase_ln(nc, S, R, dict(TT, x=TT["x0"],
                                                           fm=TT["xn0f"],
                                                           gw=TT["gw0_l"]), "A"),
         [("x0", [T, D], F32, EI), ("gw0_l", [128, 96], F32, EI),
          ("xn0f", FM_SHAPE, BF16, EO), ("route_f", [1, 8], F32, EO),
          ("route_i", [1, 2], I32, EO)],
         {"x0": "x_i", "gw0_l": "gw0_l"}),
        ("B", lambda nc, S, R, TT: phase_B(nc, S, R, dict(TT, xn=TT["xn0f"])),
         [("xn0f", FM_SHAPE, BF16, EI), ("wqT", [D, D], F32, EI),
          ("wvT", [D, D], F32, EI), ("beta0", [1, 16], F32, EI),
          ("gamma_l", [128, 16], F32, EI), ("beta_l", [128, 16], F32, EI),
          ("a0f", FM_SHAPE, BF16, EO)],
         {"xn0f": "@xn0f", "wqT": "wqT", "wvT": "wvT", "beta0": "beta0",
          "gamma_l": "gamma_l", "beta_l": "beta_l"}),
        ("C", lambda nc, S, R, TT: phase_C(nc, S, R, dict(TT, xn=TT["xn0f"])),
         [("xn0f", FM_SHAPE, BF16, EI), ("a0f", FM_SHAPE, BF16, EI),
          ("x0", [T, D], F32, EI), ("wgT", [D, D], F32, EI),
          ("woT", [D, D], F32, EI), ("acc0", [T, D], F32, EO)],
         {"xn0f": "@xn0f", "a0f": "@a0f", "x0": "x_i", "wgT": "wgT",
          "woT": "woT"}),
        ("S1", lambda nc, S, R, TT: phase_S(nc, S, R,
                                            dict(TT, xn=TT["xn0f"],
                                                 acc_in=TT["acc0"],
                                                 acc_out=TT["accA"]), 0),
         [("xn0f", FM_SHAPE, BF16, EI), ("acc0", [T, D], F32, EI),
          ("W1", [6 * D, 1408], F32, EI), ("b1", [6, 1408], F32, EI),
          ("W2", [6 * 1408, D], F32, EI), ("b2", [6, D], F32, EI),
          ("route_i", [1, 2], I32, EI), ("route_f", [1, 8], F32, EI),
          ("accA", [T, D], F32, EO)],
         {"xn0f": "@xn0f", "acc0": "@acc0", "W1": "W1f", "b1": "b1",
          "W2": "W2f", "b2": "b2", "route_i": "@route_i",
          "route_f": "@route_f"}),
        ("S2", lambda nc, S, R, TT: phase_S(nc, S, R,
                                            dict(TT, xn=TT["xn0f"],
                                                 acc_in=TT["accA"],
                                                 acc_out=TT["x1"]), 1),
         [("xn0f", FM_SHAPE, BF16, EI), ("accA", [T, D], F32, EI),
          ("W1", [6 * D, 1408], F32, EI), ("b1", [6, 1408], F32, EI),
          ("W2", [6 * 1408, D], F32, EI), ("b2", [6, D], F32, EI),
          ("route_i", [1, 2], I32, EI), ("route_f", [1, 8], F32, EI),
          ("x1", [T, D], F32, EO)],
         {"xn0f": "@xn0f", "accA": "@accA", "W1": "W1f", "b1": "b1",
          "W2": "W2f", "b2": "b2", "route_i": "@route_i",
          "route_f": "@route_f"}),
        ("E", lambda nc, S, R, TT: phase_ln(nc, S, R, dict(TT, x=TT["x1"],
                                                           fm=TT["xn1f"],
                                                           gw=TT["gw1_l"]), "E"),
         [("x1", [T, D], F32, EI), ("gw1_l", [128, 128], F32, EI),
          ("xn1f", FM_SHAPE, BF16, EO), ("comb", [T, 8], F32, EO)],
         {"x1": "@x1", "gw1_l": "gw1_l"}),
        ("F1", lambda nc, S, R, TT: phase_F1(nc, S, R, dict(TT, xn=TT["xn1f"])),
         [("xn1f", FM_SHAPE, BF16, EI), ("w_qkv", [D, 6144], F32, EI),
          ("decay1", [1, 16], F32, EI),
          ("rkvf", FM_SHAPE, BF16, EO), ("rksf", FM_SHAPE, BF16, EO)],
         {"xn1f": "@xn1f", "w_qkv": "w_qkv", "decay1": "decay1"}),
        ("F2", lambda nc, S, R, TT: phase_F2(nc, S, R, dict(TT, xn=TT["xn1f"])),
         [("xn1f", FM_SHAPE, BF16, EI), ("rkvf", FM_SHAPE, BF16, EI),
          ("rksf", FM_SHAPE, BF16, EI), ("x1", [T, D], F32, EI),
          ("w_qkv", [D, 6144], F32, EI), ("w_out", [D, D], F32, EI),
          ("den_hbm", [1, T], F32, IN), ("acc1", [T, D], F32, EO)],
         {"xn1f": "@xn1f", "rkvf": "@rkvf", "rksf": "@rksf", "x1": "@x1",
          "w_qkv": "w_qkv", "w_out": "w_out"}),
        ("G", lambda nc, S, R, TT: phase_G(nc, S, R, dict(TT, xn=TT["xn1f"])),
         [("xn1f", FM_SHAPE, BF16, EI), ("comb", [T, 8], F32, EI),
          ("acc1", [T, D], F32, EI),
          ("w1a", [9 * 2048, 1024], F32, EI), ("w3a", [9 * 2048, 1024], F32, EI),
          ("w2a", [9 * 1024, 2048], F32, EI),
          ("w1c", [9 * 2048, 1024], BF16, IN), ("w3c", [9 * 2048, 1024], BF16, IN),
          ("w2c", [9 * 1024, 2048], BF16, IN),
          ("out", [T, D], F32, EO)],
         {"xn1f": "@xn1f", "comb": "@comb", "acc1": "@acc1", "w1a": "w1a",
          "w3a": "w3a", "w2a": "w2a"}),
    ]


def _build_fused():
    nc = bass.Bass("TRN2", target_bir_lowering=False, debug=False)
    EI, EO, IN = "ExternalInput", "ExternalOutput", "Internal"
    decls = [
        ("x0", [T, D], F32, EI), ("gw0_l", [128, 96], F32, EI),
        ("wqT", [D, D], F32, EI), ("wvT", [D, D], F32, EI),
        ("wgT", [D, D], F32, EI), ("woT", [D, D], F32, EI),
        ("beta0", [1, 16], F32, EI), ("gamma_l", [128, 16], F32, EI),
        ("beta_l", [128, 16], F32, EI),
        ("W1", [6 * D, 1408], F32, EI), ("b1", [6, 1408], F32, EI),
        ("W2", [6 * 1408, D], F32, EI), ("b2", [6, D], F32, EI),
        ("gw1_l", [128, 128], F32, EI),
        ("w_qkv", [D, 6144], F32, EI), ("w_out", [D, D], F32, EI),
        ("decay1", [1, 16], F32, EI),
        ("w1a", [9 * 2048, 1024], F32, EI), ("w3a", [9 * 2048, 1024], F32, EI),
        ("w2a", [9 * 1024, 2048], F32, EI),
        ("xn0f", FM_SHAPE, BF16, IN), ("route_f", [1, 8], F32, IN),
        ("route_i", [1, 2], I32, IN), ("a0f", FM_SHAPE, BF16, IN),
        ("acc0", [T, D], F32, IN), ("accA", [T, D], F32, IN),
        ("x1", [T, D], F32, IN), ("xn1f", FM_SHAPE, BF16, IN),
        ("comb", [T, 8], F32, IN), ("rkvf", FM_SHAPE, BF16, IN),
        ("rksf", FM_SHAPE, BF16, IN), ("den_hbm", [1, T], F32, IN),
        ("acc1", [T, D], F32, IN),
        ("w1c", [9 * 2048, 1024], BF16, IN), ("w3c", [9 * 2048, 1024], BF16, IN),
        ("w2c", [9 * 1024, 2048], BF16, IN),
        ("out", [T, D], F32, EO),
    ]
    TT = {}
    for key, sh, dt, kd in decls:
        TT[key] = nc.dram_tensor(key, sh, dt, kind=kd).ap()
    with ExitStack() as ctx:
        S = Sems(nc, ctx)
        R = {"ld": DmaRing(nc, ctx, "ld", 2), "wt": DmaRing(nc, ctx, "wt", 4),
             "st": DmaRing(nc, ctx, "st", 2), "x2": DmaRing(nc, ctx, "x2", 2)}
        phase_ln(nc, S, R, dict(TT, x=TT["x0"], fm=TT["xn0f"], gw=TT["gw0_l"]),
                 "A", bar_k=0)
        phase_B(nc, S, R, dict(TT, xn=TT["xn0f"]), bar_k=1)
        phase_C(nc, S, R, dict(TT, xn=TT["xn0f"]), bar_k=2)
        phase_S(nc, S, R, dict(TT, xn=TT["xn0f"], acc_in=TT["acc0"],
                               acc_out=TT["accA"]), 0, bar_k=3)
        phase_S(nc, S, R, dict(TT, xn=TT["xn0f"], acc_in=TT["accA"],
                               acc_out=TT["x1"]), 1, bar_k=4)
        phase_ln(nc, S, R, dict(TT, x=TT["x1"], fm=TT["xn1f"], gw=TT["gw1_l"]),
                 "E", bar_k=5)
        phase_F1(nc, S, R, dict(TT, xn=TT["xn1f"]), bar_k=6)
        phase_F2(nc, S, R, dict(TT, xn=TT["xn1f"]), bar_k=7)
        phase_G(nc, S, R, dict(TT, xn=TT["xn1f"]), bar_k=8)
    return nc


def kernel(**inputs):
    P = _prep_inputs(inputs)
    P["W1f"] = P["W1"].reshape(6 * D, 1408)
    P["W2f"] = P["W2"].reshape(6 * 1408, D)
    P["w1a"] = np.ascontiguousarray(
        np.concatenate([P["ws1T"], P["w1T"].reshape(8 * D, 1024)], axis=0))
    P["w3a"] = np.ascontiguousarray(
        np.concatenate([P["ws3T"], P["w3T"].reshape(8 * D, 1024)], axis=0))
    P["w2a"] = np.ascontiguousarray(
        np.concatenate([P["ws2T"], P["w2T"].reshape(8 * 1024, D)], axis=0))
    nc = _build_fused()
    names = ["gw0_l", "wqT", "wvT", "wgT", "woT", "beta0", "gamma_l", "beta_l",
             "b1", "b2", "gw1_l", "w_qkv", "w_out", "decay1", "w1a", "w3a", "w2a"]
    base = {n: P[n] for n in names}
    base["W1"] = P["W1f"]
    base["W2"] = P["W2f"]
    in_maps = [dict(base, x0=P["x"][i]) for i in range(NCORES)]
    res = run_bass_kernel_spmd(nc, in_maps, core_ids=list(range(NCORES)))
    return np.stack([np.asarray(res.results[i]["out"]) for i in range(NCORES)],
                    axis=0)


def kernel_unfused(**inputs):
    P = _prep_inputs(inputs)
    P["W1f"] = P["W1"].reshape(6 * D, 1408)
    P["W2f"] = P["W2"].reshape(6 * 1408, D)
    P["w1a"] = np.ascontiguousarray(
        np.concatenate([P["ws1T"], P["w1T"].reshape(8 * D, 1024)], axis=0))
    P["w3a"] = np.ascontiguousarray(
        np.concatenate([P["ws3T"], P["w3T"].reshape(8 * D, 1024)], axis=0))
    P["w2a"] = np.ascontiguousarray(
        np.concatenate([P["ws2T"], P["w2T"].reshape(8 * 1024, D)], axis=0))

    core_state = [dict() for _ in range(NCORES)]
    for pname, fn, specs, srcmap in _phase_defs():
        nc, outs = _build_one(fn, [(k, sh, dt, kd) for k, sh, dt, kd in specs])
        in_maps = []
        for i in range(NCORES):
            m = {}
            for key, sh, dt, kd in specs:
                if kd != "ExternalInput":
                    continue
                srcn = srcmap[key]
                if srcn == "x_i":
                    m[key] = P["x"][i]
                elif srcn.startswith("@"):
                    m[key] = core_state[i][srcn[1:]]
                else:
                    m[key] = P[srcn]
            in_maps.append(m)
        res = run_bass_kernel_spmd(nc, in_maps, core_ids=list(range(NCORES)))
        for i in range(NCORES):
            for o in outs:
                core_state[i]["" + o] = np.asarray(res.results[i][o])
    return np.stack([core_state[i]["out"] for i in range(NCORES)], axis=0)
